# version 04
# baseline (speedup 1.0000x reference)
import os
import numpy as np
import ml_dtypes
import concourse.bass as bass
import concourse.mybir as mybir
from concourse.bass_utils import run_bass_kernel_spmd
from concourse.ap import AP as APc
from contextlib import ExitStack

F32 = mybir.dt.float32
BF16 = mybir.dt.bfloat16
I32 = mybir.dt.int32
AF = mybir.ActivationFunctionType
ALU = mybir.AluOpType
AX = mybir.AxisListType

D = 1024
L = 2048
H = 16
DK = 64
HID = 3072
NE = 8
NL = 16
NG = 4
NKB = 4
NQC = 4
NP = 8
NHH = 24
NLG = 2
EPS = 1e-5
FSC = 1.0 / 127.0

AR_TOT = 65792
A_XNT = 0
A_Q = 16384
A_K = 32768
A_V = 49152
A_PB = 0
A_AS = 5120
A_OS = 9216
A_X2T = 0
A_HB = 8192
A_WD = 32768

QC_OFF = {0: 0, 1: 640, 2: 1024, 3: 512}
QC_W = {0: 512, 1: 384, 2: 256, 3: 128}

S_SUM, S_SSQ, S_MEAN, S_VAR, S_RS, S_NB = 0, 16, 32, 48, 64, 80


def build_nc(stages=None):
    if stages is None:
        stages = int(os.environ.get("KSTAGES", "4"))
    nc = bass.Bass("TRN2", target_bir_lowering=False, detect_race_conditions=False)
    x_h = nc.dram_tensor("x", [L, D], F32, kind="ExternalInput").ap()
    wqT_h = nc.dram_tensor("wqT", [D, D], F32, kind="ExternalInput").ap()
    wkT_h = nc.dram_tensor("wkT", [D, D], F32, kind="ExternalInput").ap()
    wvT_h = nc.dram_tensor("wvT", [D, D], F32, kind="ExternalInput").ap()
    woT_h = nc.dram_tensor("woT", [D, D], F32, kind="ExternalInput").ap()
    bqk_h = nc.dram_tensor("bqk", [128, 16], F32, kind="ExternalInput").ap()
    bvr_h = nc.dram_tensor("bvr", [128, D], F32, kind="ExternalInput").ap()
    bor_h = nc.dram_tensor("bor", [128, D], F32, kind="ExternalInput").ap()
    wgT_h = nc.dram_tensor("wgT", [D, HID], I32, kind="ExternalInput").ap()
    wuT_h = nc.dram_tensor("wuT", [D, HID], I32, kind="ExternalInput").ap()
    wdT_h = nc.dram_tensor("wdT", [HID, D], I32, kind="ExternalInput").ap()
    tri_h = nc.dram_tensor("tri", [128, 128], BF16, kind="ExternalInput").ap()
    idt_h = nc.dram_tensor("idt", [128, 128], BF16, kind="ExternalInput").ap()
    x1_kind = "ExternalOutput" if os.environ.get("KDBG") else "Internal"
    x1_h = nc.dram_tensor("x1s", [L, D], F32, kind=x1_kind).ap()
    dbg_h = None
    dbg2_h = None
    if os.environ.get("KDBG"):
        dbg_h = nc.dram_tensor("dbg", [128, 16384], BF16, kind="ExternalOutput").ap()
        dbg2_h = nc.dram_tensor("dbg2", [128, 96], F32, kind="ExternalOutput").ap()
        dbg3_h = nc.dram_tensor("dbg3", [128, 2048], F32, kind="ExternalOutput").ap()
    out_h = nc.dram_tensor("out", [L, D], F32, kind="ExternalOutput").ap()

    ctx = ExitStack()
    with ctx:
        def sb(name, shape, dt=F32):
            return ctx.enter_context(nc.sbuf_tensor(name, shape, dt))

        ST = sb("ST", [128, 96], F32)
        AR = sb("AR", [128, AR_TOT], BF16)
        WA = sb("WA", [128, 16384], BF16)
        XR = sb("XR", [128, 2048], F32)
        ORR = sb("ORR", [128, 2048], F32)
        XS = sb("XS", [128, 2048], BF16)
        TRR = sb("TRR", [128, 2048], F32)
        GS = sb("GS", [128, 1024], BF16)
        ASTG = sb("ASTG", [64, 1024], BF16)
        RREP = sb("RREP", [64, 1024], BF16)
        BV = sb("BV", [128, D], BF16)
        BOr = sb("BOr", [128, D], BF16)
        BQK = sb("BQK", [128, 16], F32)
        TRI = sb("TRI", [128, 128], BF16)
        IDT = sb("IDT", [128, 128], BF16)
        ONES = sb("ONES", [1, 64], BF16)
        REC = sb("REC", [1, 1024], F32)
        RECB = sb("RECB", [1, 1024], BF16)
        SQ = sb("SQ", [128, 1024], BF16)
        PS = ctx.enter_context(nc.psum_tensor("PS", [128, 4096], F32))
        PSB = PS[:].bitcast(BF16)

        class Sem:
            def __init__(self, name):
                self.name = name
                self.h = ctx.enter_context(nc.semaphore(name))
                self.n = 0

        gsem = [Sem("g0"), Sem("g1")]
        gprev = [0, 0]
        gpar = [0]
        sx = [Sem("sx0"), Sem("sx1")]
        s_sh = Sem("s_sh")
        s_x1 = Sem("s_x1")
        s_out = Sem("s_out")
        s_dbg = Sem("s_dbg")
        t_mm = Sem("t_mm")
        sc_d = Sem("sc_d")
        ve_d = Sem("ve_d")

        Qs = {k: [] for k in ("tensor", "scalar", "vector", "gpsimd", "sync")}
        waited = {k: {} for k in Qs}

        def emit(eng, fn):
            Qs[eng].append(fn)

        def wait(eng, sem, n=None):
            v = sem.n if n is None else n
            if v <= 0 or waited[eng].get(sem.name, 0) >= v:
                return
            waited[eng][sem.name] = v
            emit(eng, lambda E, h=sem.h, vv=v: E.wait_ge(h, vv))

        own = {"vector": ve_d, "scalar": sc_d}

        def run(eng, fn, inc=None, k=1):
            o = own.get(eng)
            if o is not None:
                wait(eng, o, o.n)
                if inc is None:
                    inc, k = o, 1
            if inc is None:
                emit(eng, fn)
            else:
                emit(eng, lambda E, f=fn, h=inc.h, kk=k: f(E).then_inc(h, kk))
                inc.n += k

        ps_free = {b: [] for b in range(8)}

        def note_read(banks, sem):
            for b in banks:
                ps_free[b].append((sem, sem.n))

        def wait_banks(eng, banks):
            need = {}
            for b in banks:
                for (sem, cnt) in ps_free[b]:
                    need[sem] = max(need.get(sem, 0), cnt)
                ps_free[b] = []
            for sem, cnt in need.items():
                wait(eng, sem, cnt)

        def banks_of(c0, w):
            return set(range(c0 // 512, (c0 + w - 1) // 512 + 1))

        def ggroup(fns):
            s = gsem[gpar[0] % 2]
            wait("gpsimd", s, gprev[gpar[0] % 2])
            for f in fns:
                run("gpsimd", f, s, 16)
            gprev[gpar[0] % 2] = s.n
            gpar[0] += 1
            return (s, s.n)

        _skip = os.environ.get("KSKIP", "")
        g_const = ggroup([
            lambda E: E.dma_start(out=TRI[:], in_=tri_h),
            lambda E: E.dma_start(out=IDT[:], in_=idt_h),
            lambda E: E.dma_start(out=BQK[:], in_=bqk_h),
            lambda E: E.dma_start(out=BV[:], in_=bvr_h),
            lambda E: E.dma_start(out=BOr[:], in_=bor_h),
        ]) if "const" not in _skip else (gsem[0], 0)

        def load_w1024(src_h, slot):
            fns = []
            for half in range(2):
                def f(E, s=src_h, sl=slot, hf=half):
                    inp = s.rearrange("(dc p) e -> p dc e", p=128)[:, hf * 4:(hf + 1) * 4, :]
                    outp = WA[:, sl * 8192 + hf * 4096: sl * 8192 + (hf + 1) * 4096]
                    return E.dma_start(out=outp.rearrange("p (dc e) -> p dc e", e=1024), in_=inp)
                fns.append(f)
            return ggroup(fns)

        if "wqk" not in _skip:
            g_wq = load_w1024(wqT_h, 0)
            g_wk = load_w1024(wkT_h, 1)
        else:
            g_wq = g_wk = (gsem[0], 0)

        run("vector", lambda E: E.memset(ONES[:], 1.0), ve_d)

        def f_vones(E):
            v = AR[:, A_V:A_V + 16 * 1040].rearrange("p (t h c) -> p t h c", h=16, c=65)
            return E.memset(v[:, :, :, 64:65], 1.0)
        if "vones" not in _skip:
            run("vector", f_vones, ve_d)

        st_free = [0]
        sq_last = [0]

        xr_free = [[], []]

        def xload(tile, slot, src=x_h):
            for (sem, cnt) in xr_free[slot]:
                wait("sync", sem, cnt)
            xr_free[slot] = []
            run("sync", lambda E, t=tile, s=slot, sh=src: E.dma_start(
                out=XR[:, s * 1024:(s + 1) * 1024], in_=sh[t * 128:(t + 1) * 128, :]), sx[slot], 16)
            return (sx[slot], sx[slot].n)

        def ln_finalize(nt, f_mean=1.0 / 1024.0, f_sq=1.0 / 1024.0, rs_extra=None):
            _selfw = os.environ.get("KFINW")
            wait("vector", sc_d, st_free[0])
            run("vector", lambda E, n=nt, f=f_mean: E.tensor_scalar_mul(
                ST[:, S_MEAN:S_MEAN + n], ST[:, S_SUM:S_SUM + n], f), ve_d)
            if _selfw:
                wait("vector", ve_d)
            run("vector", lambda E, n=nt, f=f_sq: E.tensor_scalar(
                ST[:, S_VAR:S_VAR + n], ST[:, S_SSQ:S_SSQ + n], f, EPS, ALU.mult, ALU.add), ve_d)
            if _selfw:
                wait("vector", ve_d)
            run("vector", lambda E, n=nt: E.tensor_tensor(
                ST[:, S_RS:S_RS + n], ST[:, S_MEAN:S_MEAN + n], ST[:, S_MEAN:S_MEAN + n],
                ALU.mult), ve_d)
            if _selfw:
                wait("vector", ve_d)
            run("vector", lambda E, n=nt: E.tensor_tensor(
                ST[:, S_VAR:S_VAR + n], ST[:, S_VAR:S_VAR + n], ST[:, S_RS:S_RS + n],
                ALU.subtract), ve_d)
            if os.environ.get("KFIN") and nt == 16:
                wait("sync", ve_d, ve_d.n)
                run("sync", lambda E: E.dma_start(out=out_h[1408:1536, 0:96], in_=ST[:]), s_out, 16)
                wait("scalar", s_out, s_out.n)
            wait("scalar", ve_d)
            run("scalar", lambda E, n=nt: E.activation(
                out=ST[:, S_NB:S_NB + n], in_=ST[:, S_VAR:S_VAR + n], func=AF.Ln), sc_d)
            run("scalar", lambda E, n=nt: E.activation(
                out=ST[:, S_RS:S_RS + n], in_=ST[:, S_NB:S_NB + n], func=AF.Exp,
                scale=-0.5), sc_d)
            wait("vector", sc_d)
            run("vector", lambda E, n=nt: E.tensor_tensor(
                ST[:, S_NB:S_NB + n], ST[:, S_MEAN:S_MEAN + n], ST[:, S_RS:S_RS + n],
                ALU.mult), ve_d)
            run("vector", lambda E, n=nt: E.tensor_scalar_mul(
                ST[:, S_NB:S_NB + n], ST[:, S_NB:S_NB + n], -1.0), ve_d)
            if rs_extra is not None:
                run("vector", lambda E, n=nt, r=rs_extra: E.tensor_scalar_mul(
                    ST[:, S_RS:S_RS + n], ST[:, S_RS:S_RS + n], r), ve_d)

        class _StopBuild(Exception):
            pass
        try:
            for t in range(NL):
                c = xload(t, t % 2)
                if dbg2_h is not None and t == 1 and os.environ.get("KDBG2"):
                    wait("sync", sx[0], sx[0].n)
                    wait("sync", sx[1], sx[1].n)
                    run("sync", lambda E: E.dma_start(out=dbg3_h, in_=XR[:]), s_dbg, 16)
                if os.environ.get("KDBG2") and t >= 2:
                    wait("sync", s_dbg, 16)
                wait("vector", c[0], c[1])
                run("vector", lambda E, t=t, s=t % 2: E.reduce_sum(
                    ST[:, S_SUM + t:S_SUM + t + 1], XR[:, s * 1024:(s + 1) * 1024], AX.X), ve_d)
                wait("scalar", c[0], c[1])
                wait("scalar", sc_d, sq_last[0])
                run("scalar", lambda E, t=t, s=t % 2: E.activation(
                    out=SQ[:], in_=XR[:, s * 1024:(s + 1) * 1024], func=AF.Square,
                    accum_out=ST[:, S_SSQ + t:S_SSQ + t + 1]), sc_d)
                sq_last[0] = sc_d.n
                xr_free[t % 2] = [(ve_d, ve_d.n), (sc_d, sc_d.n)]

            wait("vector", sc_d)
            if "fin" not in _skip:
                ln_finalize(16)
            fin1_v = ve_d.n
            if os.environ.get("KSTD"):
                wait("sync", ve_d, ve_d.n)
                wait("sync", sc_d, sc_d.n)
                run("sync", lambda E: E.dma_start(out=out_h[0:128, 0:96], in_=ST[:]), s_out, 16)

            xs_free = [[], []]
            psb_free = [[], []]
            for t in range(NL if "p2" not in _skip else 0):
                c = xload(t, t % 2)
                slot = t % 2
                for (sem, cnt) in xs_free[slot]:
                    wait("scalar", sem, cnt)
                xs_free[slot] = []
                wait("scalar", c[0], c[1])
                wait("scalar", ve_d, fin1_v)
                run("scalar", lambda E, t=t, s=slot: E.activation(
                    out=XS[:, s * 1024:(s + 1) * 1024], in_=XR[:, s * 1024:(s + 1) * 1024],
                    func=AF.Identity, bias=ST[:, S_NB + t:S_NB + t + 1],
                    scale=ST[:, S_RS + t:S_RS + t + 1]), sc_d)
                xr_free[slot] = [(sc_d, sc_d.n)]
                wait("tensor", sc_d)
                for (sem, cnt) in psb_free[slot]:
                    wait("tensor", sem, cnt)
                psb_free[slot] = []
                for dc in range(NE):
                    run("tensor", lambda E, s=slot, dc=dc: E.transpose(
                        PSB[:, (6 + s) * 1024 + dc * 128: (6 + s) * 1024 + (dc + 1) * 128],
                        XS[:, s * 1024 + dc * 128: s * 1024 + (dc + 1) * 128], IDT[:]),
                        t_mm if dc == NE - 1 else None)
                xs_free[slot] = [(t_mm, t_mm.n)]
                if os.environ.get("KSCEVAC"):
                    wait("scalar", t_mm)

                    def fev(E, t=t, s=slot):
                        dst = APc(AR[:].tensor, t * 128, [[AR_TOT, 128], [2048, 8], [1, 128]])
                        src = PSB[:, (6 + s) * 1024:(7 + s) * 1024].rearrange("p (c l) -> p c l", l=128)
                        return E.activation(out=dst, in_=src, func=AF.Identity)
                    run("scalar", fev, sc_d)
                    psb_free[slot] = [(sc_d, sc_d.n)]
                    run("vector", lambda E: E.memset(ONES[0:1, 0:1], 1.0), ve_d)
                else:
                    wait("vector", t_mm)

                    def fev(E, t=t, s=slot):
                        dst = APc(AR[:].tensor, t * 128, [[AR_TOT, 128], [2048, 8], [1, 128]])
                        src = PSB[:, (6 + s) * 1024:(7 + s) * 1024].rearrange("p (c l) -> p c l", l=128)
                        return E.tensor_copy(dst, src)
                    run("vector", fev, ve_d)
                    psb_free[slot] = [(ve_d, ve_d.n)]
            xnt_ready = ve_d.n
            st_free[0] = sc_d.n
            if dbg_h is not None:
                wait("sync", ve_d, xnt_ready)
                run("sync", lambda E: E.dma_start(out=dbg_h[:, 0:2048],
                                                  in_=AR[:, 0:2048]), s_dbg, 16)
            if stages <= 1:
                raise _StopBuild()

            def proj_arr1(slot, dst_base, bias_col, sem_w):
                for cc in range(NE):
                    for gq in range(NG):
                        b = (cc * NG + gq) % 4
                        wait("tensor", sem_w[0], sem_w[1])
                        wait("tensor", ve_d, xnt_ready)
                        wait_banks("tensor", {b})
                        for dc in range(NE):
                            run("tensor", lambda E, cc=cc, gq=gq, dc=dc, b=b, sl=slot: E.matmul(
                                PS[:, b * 512:(b + 1) * 512],
                                WA[:, sl * 8192 + dc * 1024 + cc * 128: sl * 8192 + dc * 1024 + (cc + 1) * 128],
                                AR[:, A_XNT + dc * 2048 + gq * 512: A_XNT + dc * 2048 + (gq + 1) * 512],
                                start=(dc == 0), stop=(dc == NE - 1)),
                                t_mm if dc == NE - 1 else None)
                        wait("vector", t_mm)
                        run("vector", lambda E, cc=cc, gq=gq, b=b, dst=dst_base, bc=bias_col:
                            E.tensor_scalar_add(
                                AR[:, dst + cc * 2048 + gq * 512: dst + cc * 2048 + (gq + 1) * 512],
                                PS[:, b * 512:(b + 1) * 512], BQK[:, bc + cc:bc + cc + 1]), ve_d)
                        note_read({b}, ve_d)

            if "proj" not in _skip:
                proj_arr1(0, A_Q, 0, g_wq)
            t_q_done = t_mm.n
            if "proj" not in _skip:
                proj_arr1(1, A_K, 8, g_wk)
            t_k_done = t_mm.n

            if "wvwo" not in _skip:
                wait("gpsimd", t_mm, t_q_done)
                g_wv = load_w1024(wvT_h, 0)
                wait("gpsimd", t_mm, t_k_done)
                g_wo = load_w1024(woT_h, 1)
            else:
                g_wv = g_wo = (gsem[0], 0)

            for t in range(NL if "proj" not in _skip else 0):
                for eg in range(2):
                    b = (t * 2 + eg) % 4
                    wait("tensor", g_wv[0], g_wv[1])
                    wait_banks("tensor", {b})
                    for dc in range(NE):
                        run("tensor", lambda E, t=t, eg=eg, dc=dc, b=b: E.matmul(
                            PS[:, b * 512:(b + 1) * 512],
                            AR[:, A_XNT + dc * 2048 + t * 128: A_XNT + dc * 2048 + (t + 1) * 128],
                            WA[:, dc * 1024 + eg * 512: dc * 1024 + (eg + 1) * 512],
                            start=(dc == 0), stop=(dc == NE - 1)),
                            t_mm if dc == NE - 1 else None)
                    wait("vector", t_mm)

                    def fv(E, t=t, eg=eg, b=b):
                        dst = APc(AR[:].tensor, A_V + t * 1040 + eg * 8 * 65,
                                  [[AR_TOT, 128], [65, 8], [1, 64]])
                        src = PS[:, b * 512:(b + 1) * 512].rearrange("p (h c) -> p h c", c=64)
                        b1 = APc(BV[:].tensor, eg * 512, [[D, 128], [64, 8], [1, 64]])
                        return E.scalar_tensor_tensor(out=dst, in0=src, scalar=1.0, in1=b1,
                                                      op0=ALU.mult, op1=ALU.add)
                    run("vector", fv, ve_d)
                    note_read({b}, ve_d)
            v_all_ready = ve_d.n
            if os.environ.get("KQKV"):
                wait("gpsimd", ve_d, v_all_ready)
                run("gpsimd", lambda E: E.dma_start(out=out_h[0:128, :], in_=AR[:, A_Q:A_Q + 1024]), s_out, 16)
                run("gpsimd", lambda E: E.dma_start(out=out_h[128:256, :], in_=AR[:, A_K:A_K + 1024]), s_out, 16)
                run("gpsimd", lambda E: E.dma_start(out=out_h[256:384, :], in_=AR[:, A_V:A_V + 1024]), s_out, 16)
                run("gpsimd", lambda E: E.dma_start(out=out_h[384:512, :], in_=AR[:, A_XNT:A_XNT + 1024]), s_out, 16)
                wait("gpsimd", s_out, s_out.n)
                run("gpsimd", lambda E: E.dma_start(out=out_h[512:640, :], in_=AR[:, A_Q:A_Q + 1024]), s_out, 16)
                run("gpsimd", lambda E: E.dma_start(out=out_h[640:768, :], in_=AR[:, A_K:A_K + 1024]), s_out, 16)
                run("gpsimd", lambda E: E.dma_start(out=out_h[768:896, :], in_=AR[:, A_V:A_V + 1024]), s_out, 16)
                run("gpsimd", lambda E: E.dma_start(out=out_h[896:1024, :], in_=AR[:, A_XNT:A_XNT + 1024]), s_out, 16)
                run("gpsimd", lambda E: E.dma_start(out=out_h[1024:1152, :], in_=XS[:, 0:1024]), s_out, 16)
                run("gpsimd", lambda E: E.dma_start(out=out_h[1152:1280, :], in_=XS[:, 1024:2048]), s_out, 16)
                wait("sync", ve_d, ve_d.n)
                wait("sync", sc_d, sc_d.n)
                run("sync", lambda E: E.dma_start(out=out_h[1280:1408, 0:96], in_=ST[:]), s_out, 16)
                wait("gpsimd", s_out, s_out.n - 16)
                wait("sync", s_out, s_out.n)
                raise _StopBuild()
            if dbg_h is not None:
                wait("sync", ve_d, v_all_ready)
                run("sync", lambda E: E.dma_start(out=dbg_h[:, 2048:4096],
                                                  in_=AR[:, A_Q:A_Q + 2048]), s_dbg, 16)
                run("sync", lambda E: E.dma_start(out=dbg_h[:, 4096:6144],
                                                  in_=AR[:, A_K:A_K + 2048]), s_dbg, 16)
                run("sync", lambda E: E.dma_start(out=dbg_h[:, 6144:7184],
                                                  in_=AR[:, A_V:A_V + 1040]), s_dbg, 16)
                wait("scalar", s_dbg, s_dbg.n)
            if stages <= 2:
                raise _StopBuild()

            REG = [0, 1536]
            ATA, ATB = 3072, 3584
            reg_guard = {0: [], 1: []}
            at_guard = []
            pb_guard = {0: [], 1: [], 2: [], 3: []}
            astg_free = [[], []]
            trr_free = [[], []]
            or_free = [[], []]
            shift_cnt = 0
            x1_store_cnt = {}
            first_scores = True

            for g in range(NG):
                for p in range(NP):
                    for kb in range(NKB):
                        regA = kb % 2
                        regB = 1 - regA
                        for rg in (regA, regB):
                            for (sem, cnt) in reg_guard[rg]:
                                wait("tensor", sem, cnt)
                            reg_guard[rg] = []
                        if first_scores:
                            wait("tensor", ve_d, v_all_ready)
                            first_scores = False
                        for qc in range(NQC):
                            off = QC_OFF[qc]
                            wdt = QC_W[qc]
                            for (rg, lo, hi, tp) in ((regA, 0, 64, (0, 0)), (regB, 64, 128, (64, 0))):
                                run("tensor", lambda E, p=p, g=g, kb=kb, qc=qc, rg=rg, lo=lo, hi=hi,
                                    tp=tp, off=off, wdt=wdt: E.matmul(
                                        PS[:, REG[rg] + off: REG[rg] + off + wdt],
                                        AR[lo:hi, A_K + p * 2048 + kb * 512 + qc * 128:
                                           A_K + p * 2048 + kb * 512 + (qc + 1) * 128],
                                        AR[lo:hi, A_Q + p * 2048 + g * 512 + qc * 128:
                                           A_Q + p * 2048 + g * 512 + 512],
                                        start=True, stop=True, tile_position=tp),
                                    t_mm if (qc == NQC - 1 and rg == regB) else None)
                        t_scores = t_mm.n
                        for (rg, buf) in ((regA, kb % 2), (regB, 2 + kb % 2)):
                            for (sem, cnt) in pb_guard[buf]:
                                wait("scalar", sem, cnt)
                            pb_guard[buf] = []
                            wait("scalar", t_mm, t_scores)
                            run("scalar", lambda E, rg=rg, buf=buf: E.activation(
                                out=AR[:, A_PB + buf * 1280: A_PB + (buf + 1) * 1280],
                                in_=PS[:, REG[rg]: REG[rg] + 1280], func=AF.Exp, scale=0.125), sc_d)
                            note_read(banks_of(REG[rg], 1280), sc_d)
                            reg_guard[rg].append((sc_d, sc_d.n))
                        sc_exp = sc_d.n
                        wait("vector", sc_d, sc_exp)
                        for buf in (kb % 2, 2 + kb % 2):
                            for (mb, stride) in ((0, 640), (512, 512)):
                                def fm(E, buf=buf, mb=mb, stride=stride):
                                    vv = APc(AR[:].tensor, A_PB + buf * 1280 + mb,
                                             [[AR_TOT, 128], [stride, 2], [1, 128]])
                                    tb = TRI[:].unsqueeze(1).broadcast_to([128, 2, 128])
                                    return E.tensor_tensor(out=vv, in0=vv, in1=tb, op=ALU.mult)
                                run("vector", fm, ve_d if (mb == 512) else None)
                        ve_mask = ve_d.n
                        wait("tensor", ve_d, ve_mask)
                        if kb == 0:
                            for (sem, cnt) in at_guard:
                                wait("tensor", sem, cnt)
                            at_guard = []
                        for (hb, buf, atc) in ((0, kb % 2, ATA), (1, 2 + kb % 2, ATB)):
                            hh = 2 * p + hb
                            for qc in range(NQC):
                                jt = kb * NQC + qc
                                run("tensor", lambda E, hh=hh, jt=jt, qc=qc, buf=buf, atc=atc,
                                    kb=kb: E.matmul(
                                        PS[0:65, atc + qc * 128: atc + 512],
                                        AR[:, A_V + jt * 1040 + hh * 65: A_V + jt * 1040 + hh * 65 + 65],
                                        AR[:, A_PB + buf * 1280 + QC_OFF[qc]:
                                           A_PB + buf * 1280 + QC_OFF[qc] + QC_W[qc]],
                                        start=(kb == 0 and qc == 0),
                                        stop=(kb == NKB - 1 and qc == NQC - 1)),
                                    t_mm if (kb == NKB - 1 and qc == NQC - 1) else None)
                            pb_guard[buf] = [(t_mm, t_mm.n)]
                    t_pv = t_mm.n
                    wait("scalar", t_mm, t_pv)
                    run("scalar", lambda E: E.activation(
                        out=REC[0:1, 0:1024], in_=PS[64:65, ATA:ATA + 1024],
                        func=AF.Ln), sc_d)
                    run("scalar", lambda E: E.activation(
                        out=RECB[0:1, 0:1024], in_=REC[0:1, 0:1024],
                        func=AF.Exp, scale=-1.0), sc_d)
                    sc_rec = sc_d.n
                    wait("tensor", sc_d, sc_rec)
                    for (sem, cnt) in reg_guard[0]:
                        wait("tensor", sem, cnt)
                    reg_guard[0] = []
                    run("tensor", lambda E: E.matmul(PS[0:64, 0:512], ONES[:], RECB[0:1, 0:512],
                        start=True, stop=True), None)
                    run("tensor", lambda E: E.matmul(PS[0:64, 512:1024], ONES[:], RECB[0:1, 512:1024],
                        start=True, stop=True), t_mm)
                    t_r1 = t_mm.n
                    wait("vector", t_mm, t_r1)
                    run("vector", lambda E: E.tensor_copy(RREP[0:64, 0:512], PS[0:64, 0:512]), ve_d)
                    run("vector", lambda E: E.tensor_copy(RREP[0:64, 512:1024], PS[0:64, 512:1024]), ve_d)
                    reg_guard[0].append((ve_d, ve_d.n))
                    run("vector", lambda E, p=p: E.tensor_tensor(
                        out=AR[0:64, A_AS + p * 512: A_AS + (p + 1) * 512],
                        in0=PS[0:64, ATA:ATA + 512], in1=RREP[0:64, 0:512], op=ALU.mult), ve_d)
                    for (sem, cnt) in astg_free[p % 2]:
                        wait("vector", sem, cnt)
                    astg_free[p % 2] = []
                    run("vector", lambda E, p=p: E.tensor_tensor(
                        out=ASTG[0:64, (p % 2) * 512:(p % 2 + 1) * 512],
                        in0=PS[0:64, ATB:ATB + 512], in1=RREP[0:64, 512:1024], op=ALU.mult), ve_d)
                    at_guard = [(ve_d, ve_d.n)]
                    ve_norm = ve_d.n
                    wait("sync", ve_d, ve_norm)
                    run("sync", lambda E, p=p: E.dma_start(
                        out=AR[64:128, A_AS + p * 512: A_AS + (p + 1) * 512],
                        in_=ASTG[0:64, (p % 2) * 512:(p % 2 + 1) * 512]), s_sh, 16)
                    shift_cnt = s_sh.n
                    astg_free[p % 2] = [(s_sh, s_sh.n)]

                wait("tensor", s_sh, shift_cnt)
                wait("tensor", g_wo[0], g_wo[1])
                for rg in (0, 1):
                    for (sem, cnt) in reg_guard[rg]:
                        wait("tensor", sem, cnt)
                    reg_guard[rg] = []
                for (sem, cnt) in at_guard:
                    wait("tensor", sem, cnt)
                at_guard = []
                wait_banks("tensor", set(range(8)))
                for s in range(4):
                    for eg in range(2):
                        b = s * 2 + eg
                        for cc in range(NP):
                            run("tensor", lambda E, s=s, eg=eg, cc=cc, b=b: E.matmul(
                                PS[:, b * 512:(b + 1) * 512],
                                AR[:, A_AS + cc * 512 + s * 128: A_AS + cc * 512 + (s + 1) * 128],
                                WA[:, 8192 + cc * 1024 + eg * 512: 8192 + cc * 1024 + (eg + 1) * 512],
                                start=(cc == 0), stop=(cc == NP - 1)),
                                t_mm if cc == NP - 1 else None)
                        wait("vector", t_mm)

                        def fo(E, s=s, eg=eg, b=b):
                            return E.scalar_tensor_tensor(
                                out=AR[:, A_OS + s * 1024 + eg * 512: A_OS + s * 1024 + (eg + 1) * 512],
                                in0=PS[:, b * 512:(b + 1) * 512], scalar=1.0,
                                in1=BOr[:, eg * 512:(eg + 1) * 512], op0=ALU.mult, op1=ALU.add)
                        run("vector", fo, ve_d)
                        note_read({b}, ve_d)
                ve_oevac = ve_d.n
                reg_guard[0].append((ve_d, ve_oevac))
                reg_guard[1].append((ve_d, ve_oevac))
                at_guard.append((ve_d, ve_oevac))

                wait("vector", ve_d, ve_oevac)
                for s in range(4):
                    run("vector", lambda E, s=s: E.reduce_sum(
                        ST[:, S_SUM + s:S_SUM + s + 1],
                        AR[:, A_OS + s * 1024: A_OS + (s + 1) * 1024], AX.X), ve_d)
                wait("scalar", ve_d, ve_oevac)
                for s in range(4):
                    wait("scalar", sc_d, sq_last[0])
                    run("scalar", lambda E, s=s: E.activation(
                        out=SQ[:], in_=AR[:, A_OS + s * 1024: A_OS + (s + 1) * 1024],
                        func=AF.Square, accum_out=ST[:, S_SSQ + s:S_SSQ + s + 1]), sc_d)
                    sq_last[0] = sc_d.n
                wait("vector", sc_d)
                ln_finalize(4)
                fin2_v = ve_d.n
                for s in range(4):
                    t_ = 4 * g + s
                    c = xload(t_, s % 2)
                    for (sem, cnt) in trr_free[s % 2]:
                        wait("scalar", sem, cnt)
                    trr_free[s % 2] = []
                    wait("scalar", ve_d, fin2_v)
                    run("scalar", lambda E, s=s: E.activation(
                        out=TRR[:, (s % 2) * 1024:((s % 2) + 1) * 1024],
                        in_=AR[:, A_OS + s * 1024: A_OS + (s + 1) * 1024],
                        func=AF.Identity, bias=ST[:, S_NB + s:S_NB + s + 1],
                        scale=ST[:, S_RS + s:S_RS + s + 1]), sc_d)
                    wait("vector", c[0], c[1])
                    wait("vector", sc_d)
                    for (sem, cnt) in or_free[s % 2]:
                        wait("vector", sem, cnt)
                    or_free[s % 2] = []
                    run("vector", lambda E, s=s: E.tensor_tensor(
                        out=ORR[:, (s % 2) * 1024:((s % 2) + 1) * 1024],
                        in0=XR[:, (s % 2) * 1024:((s % 2) + 1) * 1024],
                        in1=TRR[:, (s % 2) * 1024:((s % 2) + 1) * 1024], op=ALU.add), ve_d)
                    xr_free[s % 2] = [(ve_d, ve_d.n)]
                    trr_free[s % 2] = [(ve_d, ve_d.n)]
                    wait("sync", ve_d)
                    run("sync", lambda E, t_=t_, s=s: E.dma_start(
                        out=x1_h[t_ * 128:(t_ + 1) * 128, :],
                        in_=ORR[:, (s % 2) * 1024:((s % 2) + 1) * 1024]), s_x1, 16)
                    x1_store_cnt[t_] = s_x1.n
                    or_free[s % 2] = [(s_x1, s_x1.n)]
                st_free[0] = sc_d.n

            t_attn_done = t_mm.n
            attn_ve_done = ve_d.n
            attn_sc_done = sc_d.n
            if stages <= 3:
                wait("sync", s_x1, s_x1.n)
                run("sync", lambda E: E.dma_start(out=out_h, in_=x1_h), s_out, 16)
                raise _StopBuild()

            wait("gpsimd", t_mm, t_attn_done)
            _fwd = []
            for i6 in range(6):
                def fwd(E, i6=i6):
                    inp = wdT_h.rearrange("(hh p) d -> p hh d", p=128)[:, i6 * 4:(i6 + 1) * 4, :]
                    outp = AR[:, A_WD + i6 * 4096: A_WD + (i6 + 1) * 4096]
                    return E.dma_start(out=outp.rearrange("p (hh d) -> p hh d", d=1024), in_=inp)
                _fwd.append(fwd)
            g_wd = ggroup(_fwd)

            wait("tensor", ve_d, attn_ve_done)
            wait("tensor", sc_d, attn_sc_done)

            wa_guard = {r: [] for r in range(16)}
            wa_init = [(t_mm, t_attn_done)]
            gs_free = [[], []]

            for lg in range(NLG):
                for t in range(8):
                    lt = lg * 8 + t
                    wait("sync", s_x1, x1_store_cnt[lt])
                    c = xload(lt, t % 2, src=x1_h)
                    wait("vector", c[0], c[1])
                    run("vector", lambda E, t=t, s=t % 2: E.reduce_sum(
                        ST[:, S_SUM + t:S_SUM + t + 1], XR[:, s * 1024:(s + 1) * 1024], AX.X), ve_d)
                    wait("scalar", c[0], c[1])
                    wait("scalar", sc_d, sq_last[0])
                    run("scalar", lambda E, t=t, s=t % 2: E.activation(
                        out=SQ[:], in_=XR[:, s * 1024:(s + 1) * 1024], func=AF.Square,
                        accum_out=ST[:, S_SSQ + t:S_SSQ + t + 1]), sc_d)
                    sq_last[0] = sc_d.n
                    xr_free[t % 2] = [(ve_d, ve_d.n), (sc_d, sc_d.n)]
                wait("vector", sc_d)
                ln_finalize(8)
                fin3_v = ve_d.n
                for t in range(8):
                    lt = lg * 8 + t
                    c = xload(lt, t % 2, src=x1_h)
                    slot = t % 2
                    for (sem, cnt) in xs_free[slot]:
                        wait("scalar", sem, cnt)
                    xs_free[slot] = []
                    wait("scalar", c[0], c[1])
                    wait("scalar", ve_d, fin3_v)
                    run("scalar", lambda E, t=t, s=slot: E.activation(
                        out=XS[:, s * 1024:(s + 1) * 1024], in_=XR[:, s * 1024:(s + 1) * 1024],
                        func=AF.Identity, bias=ST[:, S_NB + t:S_NB + t + 1],
                        scale=ST[:, S_RS + t:S_RS + t + 1]), sc_d)
                    xr_free[slot] = [(sc_d, sc_d.n)]
                    wait("tensor", sc_d)
                    for (sem, cnt) in psb_free[slot]:
                        wait("tensor", sem, cnt)
                    psb_free[slot] = []
                    wait_banks("tensor", {slot})
                    for dc in range(NE):
                        run("tensor", lambda E, s=slot, dc=dc: E.transpose(
                            PSB[:, s * 1024 + dc * 128: s * 1024 + (dc + 1) * 128],
                            XS[:, s * 1024 + dc * 128: s * 1024 + (dc + 1) * 128], IDT[:]),
                            t_mm if dc == NE - 1 else None)
                    xs_free[slot] = [(t_mm, t_mm.n)]
                    wait("vector", t_mm)

                    def fev2(E, t=t, s=slot):
                        dst = APc(AR[:].tensor, A_X2T + t * 128, [[AR_TOT, 128], [1024, 8], [1, 128]])
                        src = PSB[:, s * 1024:(s + 1) * 1024].rearrange("p (c l) -> p c l", l=128)
                        return E.tensor_copy(dst, src)
                    run("vector", fev2, ve_d)
                    psb_free[slot] = [(ve_d, ve_d.n)]
                    note_read({slot}, ve_d)
                x2t_ready = ve_d.n
                st_free[0] = sc_d.n
                if os.environ.get("KSUB", "full") == "ln3":
                    raise _StopBuild()

                h_ready = {}
                for hh in range(NHH):
                    rg_ = hh % 8
                    ru_ = 8 + hh % 8
                    for r_ in (rg_, ru_):
                        for (sem, cnt) in (wa_guard[r_] or wa_init):
                            wait("gpsimd", sem, cnt)
                        wa_guard[r_] = []
                    _fsl = []
                    for (src_h, r_) in ((wgT_h, rg_), (wuT_h, ru_)):
                        for half in range(2):
                            def fsl(E, sh=src_h, r=r_, hf=half, hh=hh):
                                inp = sh.rearrange("(dc p) h -> p dc h", p=128)[
                                    :, hf * 4:(hf + 1) * 4, hh * 128:(hh + 1) * 128]
                                outp = WA[:, r * 1024 + hf * 512: r * 1024 + (hf + 1) * 512]
                                return E.dma_start(
                                    out=outp.rearrange("p (dc e) -> p dc e", e=128), in_=inp)
                            _fsl.append(fsl)
                    g_sl = ggroup(_fsl)
                    for l5 in range(2):
                        bg = (hh * 2 + l5) % 2
                        bu = 2 + bg
                        wait("tensor", g_sl[0], g_sl[1])
                        wait("tensor", ve_d, x2t_ready)
                        wait_banks("tensor", {bg, bu})
                        for dc in range(NE):
                            run("tensor", lambda E, dc=dc, bg=bg, r=rg_, l5=l5: E.matmul(
                                PS[:, bg * 512:(bg + 1) * 512],
                                WA[:, r * 1024 + dc * 128: r * 1024 + (dc + 1) * 128],
                                AR[:, A_X2T + dc * 1024 + l5 * 512: A_X2T + dc * 1024 + (l5 + 1) * 512],
                                start=(dc == 0), stop=(dc == NE - 1)),
                                t_mm if dc == NE - 1 else None)
                        t_g = t_mm.n
                        for dc in range(NE):
                            run("tensor", lambda E, dc=dc, bu=bu, r=ru_, l5=l5: E.matmul(
                                PS[:, bu * 512:(bu + 1) * 512],
                                WA[:, r * 1024 + dc * 128: r * 1024 + (dc + 1) * 128],
                                AR[:, A_X2T + dc * 1024 + l5 * 512: A_X2T + dc * 1024 + (l5 + 1) * 512],
                                start=(dc == 0), stop=(dc == NE - 1)),
                                t_mm if dc == NE - 1 else None)
                        t_u = t_mm.n
                        wa_guard[rg_] = [(t_mm, t_g)]
                        wa_guard[ru_] = [(t_mm, t_u)]
                        gslot = (hh * 2 + l5) % 2
                        for (sem, cnt) in gs_free[gslot]:
                            wait("scalar", sem, cnt)
                        gs_free[gslot] = []
                        wait("scalar", t_mm, t_g)
                        run("scalar", lambda E, bg=bg, gs=gslot: E.activation(
                            out=GS[:, gs * 512:(gs + 1) * 512], in_=PS[:, bg * 512:(bg + 1) * 512],
                            func=AF.Silu, scale=FSC), sc_d)
                        note_read({bg}, sc_d)
                        wait("vector", sc_d)
                        wait("vector", t_mm, t_u)
                        run("vector", lambda E, hh=hh, l5=l5, bu=bu, gs=gslot: E.scalar_tensor_tensor(
                            out=AR[:, A_HB + hh * 1024 + l5 * 512: A_HB + hh * 1024 + (l5 + 1) * 512],
                            in0=PS[:, bu * 512:(bu + 1) * 512], scalar=FSC,
                            in1=GS[:, gs * 512:(gs + 1) * 512], op0=ALU.mult, op1=ALU.mult), ve_d)
                        note_read({bu}, ve_d)
                        gs_free[gslot] = [(ve_d, ve_d.n)]
                        h_ready[(hh, l5)] = ve_d.n

                if os.environ.get("KSUB", "full") == "gate":
                    raise _StopBuild()
                wait("tensor", g_wd[0], g_wd[1])
                for l5 in range(2):
                    wait_banks("tensor", set(range(8)))
                    for hh in range(NHH):
                        wait("tensor", ve_d, h_ready[(hh, l5)])
                        for s in range(4):
                            for dg in range(2):
                                b = s * 2 + dg
                                run("tensor", lambda E, hh=hh, s=s, dg=dg, b=b, l5=l5: E.matmul(
                                    PS[:, b * 512:(b + 1) * 512],
                                    AR[:, A_HB + hh * 1024 + l5 * 512 + s * 128:
                                       A_HB + hh * 1024 + l5 * 512 + (s + 1) * 128],
                                    AR[:, A_WD + hh * 1024 + dg * 512: A_WD + hh * 1024 + (dg + 1) * 512],
                                    start=(hh == 0), stop=(hh == NHH - 1)),
                                    t_mm if (hh == NHH - 1 and b == 7) else None)
                    t_down = t_mm.n
                    if os.environ.get("KSUB", "full") == "downmm":
                        wait("vector", t_mm, t_down)
                        run("vector", lambda E: E.memset(ST[:, 0:1], 0.0), ve_d)
                        raise _StopBuild()
                    wait("vector", t_mm, t_down)
                    for s in range(4):
                        run("vector", lambda E, s=s: E.reduce_sum(
                            ST[:, S_SUM + s:S_SUM + s + 1], PS[:, s * 1024:(s + 1) * 1024], AX.X), ve_d)
                    if os.environ.get("KSUB", "full") == "red":
                        raise _StopBuild()
                    wait("scalar", t_mm, t_down)
                    wait("scalar", ve_d)
                    for s in range(4):
                        wait("scalar", sc_d, sq_last[0])
                        run("scalar", lambda E, s=s: E.activation(
                            out=SQ[:], in_=PS[:, s * 1024:(s + 1) * 1024], func=AF.Square,
                            scale=FSC, accum_out=ST[:, S_SSQ + s:S_SSQ + s + 1]), sc_d)
                        sq_last[0] = sc_d.n
                    if os.environ.get("KSUB", "full") == "sq":
                        raise _StopBuild()
                    wait("vector", sc_d)
                    ln_finalize(4, f_mean=FSC / 1024.0, f_sq=1.0 / 1024.0, rs_extra=FSC)
                    fin4_v = ve_d.n
                    if os.environ.get("KSUB", "full") == "ln4":
                        raise _StopBuild()
                    for s in range(4):
                        lt = lg * 8 + l5 * 4 + s
                        c = xload(lt, s % 2, src=x1_h)
                        for (sem, cnt) in trr_free[s % 2]:
                            wait("scalar", sem, cnt)
                        trr_free[s % 2] = []
                        wait("scalar", ve_d, fin4_v)
                        run("scalar", lambda E, s=s: E.activation(
                            out=TRR[:, (s % 2) * 1024:((s % 2) + 1) * 1024],
                            in_=PS[:, s * 1024:(s + 1) * 1024],
                            func=AF.Identity, bias=ST[:, S_NB + s:S_NB + s + 1],
                            scale=ST[:, S_RS + s:S_RS + s + 1]), sc_d)
                        note_read({s * 2, s * 2 + 1}, sc_d)
                        wait("vector", c[0], c[1])
                        wait("vector", sc_d)
                        for (sem, cnt) in or_free[s % 2]:
                            wait("vector", sem, cnt)
                        or_free[s % 2] = []
                        run("vector", lambda E, s=s: E.tensor_tensor(
                            out=ORR[:, (s % 2) * 1024:((s % 2) + 1) * 1024],
                            in0=XR[:, (s % 2) * 1024:((s % 2) + 1) * 1024],
                            in1=TRR[:, (s % 2) * 1024:((s % 2) + 1) * 1024], op=ALU.add), ve_d)
                        xr_free[s % 2] = [(ve_d, ve_d.n)]
                        trr_free[s % 2] = [(ve_d, ve_d.n)]
                        wait("sync", ve_d)
                        run("sync", lambda E, lt=lt, s=s: E.dma_start(
                            out=out_h[lt * 128:(lt + 1) * 128, :],
                            in_=ORR[:, (s % 2) * 1024:((s % 2) + 1) * 1024]), s_out, 16)
                        or_free[s % 2] = [(s_out, s_out.n)]
                    st_free[0] = sc_d.n

        except _StopBuild:
            pass
        wait("sync", s_dbg, s_dbg.n)
        wait("sync", s_out, s_out.n)
        wait("sync", s_x1, s_x1.n)
        wait("sync", s_sh, s_sh.n)
        wait("sync", sx[0], sx[0].n)
        wait("sync", sx[1], sx[1].n)
        wait("gpsimd", gsem[0], gsem[0].n)
        wait("gpsimd", gsem[1], gsem[1].n)

        block = ctx.enter_context(nc.Block())

        @block.tensor
        def _(E):
            for f in Qs["tensor"]:
                f(E)

        @block.scalar
        def _(E):
            for f in Qs["scalar"]:
                f(E)

        @block.vector
        def _(E):
            for f in Qs["vector"]:
                f(E)

        @block.gpsimd
        def _(E):
            for f in Qs["gpsimd"]:
                f(E)

        @block.sync
        def _(E):
            for f in Qs["sync"]:
                f(E)

    return nc


_NC_CACHE = None


def _get_nc():
    global _NC_CACHE
    if _NC_CACHE is None:
        _NC_CACHE = build_nc()
    return _NC_CACHE


def make_inputs(**inputs):
    x = np.asarray(inputs["L0_attn_x"], dtype=np.float32)
    wq = np.asarray(inputs["L0_attn_Wq"], dtype=np.float32)
    bq = np.asarray(inputs["L0_attn_bq"], dtype=np.float32)
    wk = np.asarray(inputs["L0_attn_Wk"], dtype=np.float32)
    bk = np.asarray(inputs["L0_attn_bk"], dtype=np.float32)
    wv = np.asarray(inputs["L0_attn_Wv"], dtype=np.float32)
    bv = np.asarray(inputs["L0_attn_bv"], dtype=np.float32)
    wo = np.asarray(inputs["L0_attn_Wo"], dtype=np.float32)
    bo = np.asarray(inputs["L0_attn_bo"], dtype=np.float32)
    wg = np.asarray(inputs["L0_ffn_w_gate"], dtype=np.int32)
    wu = np.asarray(inputs["L0_ffn_w_up"], dtype=np.int32)
    wd = np.asarray(inputs["L0_ffn_w_down"], dtype=np.int32)

    shared = {
        "wqT": np.ascontiguousarray(wq.T),
        "wkT": np.ascontiguousarray(wk.T),
        "wvT": np.ascontiguousarray(wv.T),
        "woT": np.ascontiguousarray(wo.T),
        "bqk": np.ascontiguousarray(
            np.concatenate([bq.reshape(8, 128).T, bk.reshape(8, 128).T], axis=1)),
        "bvr": np.ascontiguousarray(np.broadcast_to(bv[None, :], (128, D))),
        "bor": np.ascontiguousarray(np.broadcast_to(bo[None, :], (128, D))),
        "wgT": np.ascontiguousarray(wg.T),
        "wuT": np.ascontiguousarray(wu.T),
        "wdT": np.ascontiguousarray(wd.T),
        "tri": np.triu(np.ones((128, 128), dtype=np.float32)).astype(ml_dtypes.bfloat16),
        "idt": np.eye(128, dtype=ml_dtypes.bfloat16),
    }
    in_maps = []
    for b in range(8):
        m = dict(shared)
        m["x"] = np.ascontiguousarray(x[b])
        in_maps.append(m)
    return in_maps


def kernel(**inputs):
    in_maps = make_inputs(**inputs)
    nc = _get_nc()
    res = run_bass_kernel_spmd(nc, in_maps, core_ids=list(range(8)))
    out = np.stack([np.asarray(r["out"], dtype=np.float32) for r in res.results], axis=0)
    return out


# version 06
# speedup vs baseline: 1.2019x; 1.2019x over previous
import os
import numpy as np
import ml_dtypes
import concourse.bass as bass
import concourse.mybir as mybir
from concourse.bass_utils import run_bass_kernel_spmd
from concourse.ap import AP as APc
from contextlib import ExitStack

F32 = mybir.dt.float32
BF16 = mybir.dt.bfloat16
I32 = mybir.dt.int32
AF = mybir.ActivationFunctionType
ALU = mybir.AluOpType
AX = mybir.AxisListType

D = 1024
L = 2048
H = 16
DK = 64
HID = 3072
NE = 8
NL = 16
NG = 4
NKB = 4
NQC = 4
NP = 8
NHH = 24
NLG = 2
EPS = 1e-5
FSC = 1.0 / 127.0

AR_TOT = 65792
A_XNT = 0
A_Q = 16384
A_K = 32768
A_V = 49152
A_PB = 0
A_AS = 5120
A_OS = 9216
A_X2T = 0
A_HB = 8192
A_WD = 32768

QC_OFF = {0: 0, 1: 640, 2: 1024, 3: 512}
QC_W = {0: 512, 1: 384, 2: 256, 3: 128}

S_SUM, S_SSQ, S_MEAN, S_VAR, S_RS, S_NB = 0, 16, 32, 48, 64, 80


def build_nc(stages=None):
    if stages is None:
        stages = int(os.environ.get("KSTAGES", "4"))
    nc = bass.Bass("TRN2", target_bir_lowering=False, detect_race_conditions=False)
    x_h = nc.dram_tensor("x", [L, D], F32, kind="ExternalInput").ap()
    wqT_h = nc.dram_tensor("wqT", [D, D], F32, kind="ExternalInput").ap()
    wkT_h = nc.dram_tensor("wkT", [D, D], F32, kind="ExternalInput").ap()
    wvT_h = nc.dram_tensor("wvT", [D, D], F32, kind="ExternalInput").ap()
    woT_h = nc.dram_tensor("woT", [D, D], F32, kind="ExternalInput").ap()
    bqk_h = nc.dram_tensor("bqk", [128, 16], F32, kind="ExternalInput").ap()
    bvr_h = nc.dram_tensor("bvr", [128, D], F32, kind="ExternalInput").ap()
    bor_h = nc.dram_tensor("bor", [128, D], F32, kind="ExternalInput").ap()
    wgT_h = nc.dram_tensor("wgT", [D, HID], I32, kind="ExternalInput").ap()
    wuT_h = nc.dram_tensor("wuT", [D, HID], I32, kind="ExternalInput").ap()
    wdT_h = nc.dram_tensor("wdT", [HID, D], I32, kind="ExternalInput").ap()
    tri_h = nc.dram_tensor("tri", [128, 128], BF16, kind="ExternalInput").ap()
    idt_h = nc.dram_tensor("idt", [128, 128], BF16, kind="ExternalInput").ap()
    x1_kind = "ExternalOutput" if os.environ.get("KDBG") else "Internal"
    x1_h = nc.dram_tensor("x1s", [L, D], F32, kind=x1_kind).ap()
    dbg_h = None
    dbg2_h = None
    if os.environ.get("KDBG"):
        dbg_h = nc.dram_tensor("dbg", [128, 16384], BF16, kind="ExternalOutput").ap()
        dbg2_h = nc.dram_tensor("dbg2", [128, 96], F32, kind="ExternalOutput").ap()
        dbg3_h = nc.dram_tensor("dbg3", [128, 2048], F32, kind="ExternalOutput").ap()
    out_h = nc.dram_tensor("out", [L, D], F32, kind="ExternalOutput").ap()

    ctx = ExitStack()
    with ctx:
        def sb(name, shape, dt=F32):
            return ctx.enter_context(nc.sbuf_tensor(name, shape, dt))

        ST = sb("ST", [128, 96], F32)
        AR = sb("AR", [128, AR_TOT], BF16)
        WA = sb("WA", [128, 16384], BF16)
        XR = sb("XR", [128, 3072], F32)
        ORR = sb("ORR", [128, 2048], F32)
        XS = sb("XS", [128, 2048], BF16)
        TRR = sb("TRR", [128, 2048], BF16)
        GS = sb("GS", [128, 1024], BF16)
        ASTG = sb("ASTG", [64, 1024], BF16)
        RREP = sb("RREP", [64, 1024], BF16)
        BV = sb("BV", [128, D], BF16)
        BOr = sb("BOr", [128, D], BF16)
        BQK = sb("BQK", [128, 16], F32)
        TRI = sb("TRI", [128, 128], BF16)
        IDT = sb("IDT", [128, 128], BF16)
        ONES = sb("ONES", [1, 64], BF16)
        REC = sb("REC", [1, 1024], F32)
        RECB = sb("RECB", [1, 1024], BF16)
        SQ = sb("SQ", [128, 1024], BF16)
        PS = ctx.enter_context(nc.psum_tensor("PS", [128, 4096], F32))
        PSB = PS[:].bitcast(BF16)

        class Sem:
            def __init__(self, name):
                self.name = name
                self.h = ctx.enter_context(nc.semaphore(name))
                self.n = 0

        gsem = [Sem("g0"), Sem("g1"), Sem("g2"), Sem("g3")]
        gprev = [0, 0, 0, 0]
        gpar = [0]
        sx = [Sem("sx0"), Sem("sx1"), Sem("sx2")]
        s_sh = Sem("s_sh")
        s_x1 = Sem("s_x1")
        s_out = Sem("s_out")
        s_dbg = Sem("s_dbg")
        t_mm = Sem("t_mm")
        sc_d = Sem("sc_d")
        ve_d = Sem("ve_d")

        Qs = {k: [] for k in ("tensor", "scalar", "vector", "gpsimd", "sync")}
        waited = {k: {} for k in Qs}

        def emit(eng, fn):
            Qs[eng].append(fn)

        def wait(eng, sem, n=None):
            v = sem.n if n is None else n
            if v <= 0 or waited[eng].get(sem.name, 0) >= v:
                return
            waited[eng][sem.name] = v
            emit(eng, lambda E, h=sem.h, vv=v: E.wait_ge(h, vv))

        own = {"vector": ve_d, "scalar": sc_d}

        def run(eng, fn, inc=None, k=1):
            o = own.get(eng)
            if o is not None:
                wait(eng, o, o.n)
                if inc is None:
                    inc, k = o, 1
            if inc is None:
                emit(eng, fn)
            else:
                emit(eng, lambda E, f=fn, h=inc.h, kk=k: f(E).then_inc(h, kk))
                inc.n += k

        ps_free = {b: [] for b in range(8)}

        def note_read(banks, sem):
            for b in banks:
                ps_free[b].append((sem, sem.n))

        def wait_banks(eng, banks):
            need = {}
            for b in banks:
                for (sem, cnt) in ps_free[b]:
                    need[sem] = max(need.get(sem, 0), cnt)
                ps_free[b] = []
            for sem, cnt in need.items():
                wait(eng, sem, cnt)

        def banks_of(c0, w):
            return set(range(c0 // 512, (c0 + w - 1) // 512 + 1))

        def heartbeat():
            if os.environ.get("KNOHB"):
                return
            wait("tensor", sc_d, sc_d.n)
            run("tensor", lambda E: E.matmul(PS[0:64, 2048:2112], ONES[:], ONES[:],
                                             start=True, stop=True), None)

        def ggroup(fns):
            s = gsem[gpar[0] % 4]
            wait("gpsimd", s, gprev[gpar[0] % 4])
            for f in fns:
                run("gpsimd", f, s, 16)
            gprev[gpar[0] % 4] = s.n
            gpar[0] += 1
            return (s, s.n)

        _skip = os.environ.get("KSKIP", "")
        g_const = ggroup([
            lambda E: E.dma_start(out=TRI[:], in_=tri_h),
            lambda E: E.dma_start(out=IDT[:], in_=idt_h),
            lambda E: E.dma_start(out=BQK[:], in_=bqk_h),
            lambda E: E.dma_start(out=BV[:], in_=bvr_h),
            lambda E: E.dma_start(out=BOr[:], in_=bor_h),
        ]) if "const" not in _skip else (gsem[0], 0)

        def load_w1024(src_h, slot):
            fns = []
            for half in range(2):
                def f(E, s=src_h, sl=slot, hf=half):
                    inp = s.rearrange("(dc p) e -> p dc e", p=128)[:, hf * 4:(hf + 1) * 4, :]
                    outp = WA[:, sl * 8192 + hf * 4096: sl * 8192 + (hf + 1) * 4096]
                    return E.dma_start(out=outp.rearrange("p (dc e) -> p dc e", e=1024), in_=inp)
                fns.append(f)
            return ggroup(fns)

        if "wqk" not in _skip:
            g_wq = load_w1024(wqT_h, 0)
            g_wk = load_w1024(wkT_h, 1)
        else:
            g_wq = g_wk = (gsem[0], 0)

        run("vector", lambda E: E.memset(ONES[:], 1.0), ve_d)

        def f_vones(E):
            v = AR[:, A_V:A_V + 16 * 1040].rearrange("p (t h c) -> p t h c", h=16, c=65)
            return E.memset(v[:, :, :, 64:65], 1.0)
        if "vones" not in _skip:
            run("vector", f_vones, ve_d)

        st_free = [0]
        sq_last = [0]

        xr_free = [[], [], []]

        def xload(tile, slot, src=x_h):
            for (sem, cnt) in xr_free[slot]:
                wait("sync", sem, cnt)
            xr_free[slot] = []
            run("sync", lambda E, t=tile, s=slot, sh=src: E.dma_start(
                out=XR[:, s * 1024:(s + 1) * 1024], in_=sh[t * 128:(t + 1) * 128, :]), sx[slot], 16)
            return (sx[slot], sx[slot].n)

        def ln_finalize(nt, f_mean=1.0 / 1024.0, f_sq=1.0 / 1024.0, rs_extra=None):
            _selfw = os.environ.get("KFINW")
            wait("vector", sc_d, st_free[0])
            run("vector", lambda E, n=nt, f=f_mean: E.tensor_scalar_mul(
                ST[:, S_MEAN:S_MEAN + n], ST[:, S_SUM:S_SUM + n], f), ve_d)
            if _selfw:
                wait("vector", ve_d)
            run("vector", lambda E, n=nt, f=f_sq: E.tensor_scalar(
                ST[:, S_VAR:S_VAR + n], ST[:, S_SSQ:S_SSQ + n], f, EPS, ALU.mult, ALU.add), ve_d)
            if _selfw:
                wait("vector", ve_d)
            run("vector", lambda E, n=nt: E.tensor_tensor(
                ST[:, S_RS:S_RS + n], ST[:, S_MEAN:S_MEAN + n], ST[:, S_MEAN:S_MEAN + n],
                ALU.mult), ve_d)
            if _selfw:
                wait("vector", ve_d)
            run("vector", lambda E, n=nt: E.tensor_tensor(
                ST[:, S_VAR:S_VAR + n], ST[:, S_VAR:S_VAR + n], ST[:, S_RS:S_RS + n],
                ALU.subtract), ve_d)
            if os.environ.get("KFIN") and nt == 16:
                wait("sync", ve_d, ve_d.n)
                run("sync", lambda E: E.dma_start(out=out_h[1408:1536, 0:96], in_=ST[:]), s_out, 16)
                wait("scalar", s_out, s_out.n)
            wait("scalar", ve_d)
            run("scalar", lambda E, n=nt: E.activation(
                out=ST[:, S_NB:S_NB + n], in_=ST[:, S_VAR:S_VAR + n], func=AF.Ln), sc_d)
            run("scalar", lambda E, n=nt: E.activation(
                out=ST[:, S_RS:S_RS + n], in_=ST[:, S_NB:S_NB + n], func=AF.Exp,
                scale=-0.5), sc_d)
            wait("vector", sc_d)
            run("vector", lambda E, n=nt: E.tensor_tensor(
                ST[:, S_NB:S_NB + n], ST[:, S_MEAN:S_MEAN + n], ST[:, S_RS:S_RS + n],
                ALU.mult), ve_d)
            run("vector", lambda E, n=nt: E.tensor_scalar_mul(
                ST[:, S_NB:S_NB + n], ST[:, S_NB:S_NB + n], -1.0), ve_d)
            if rs_extra is not None:
                run("vector", lambda E, n=nt, r=rs_extra: E.tensor_scalar_mul(
                    ST[:, S_RS:S_RS + n], ST[:, S_RS:S_RS + n], r), ve_d)

        class _StopBuild(Exception):
            pass
        try:
            for t in range(NL):
                c = xload(t, t % 3)
                if dbg2_h is not None and t == 1 and os.environ.get("KDBG2"):
                    wait("sync", sx[0], sx[0].n)
                    wait("sync", sx[1], sx[1].n)
                    run("sync", lambda E: E.dma_start(out=dbg3_h, in_=XR[:]), s_dbg, 16)
                if os.environ.get("KDBG2") and t >= 2:
                    wait("sync", s_dbg, 16)
                wait("vector", c[0], c[1])
                run("vector", lambda E, t=t, s=t % 3: E.reduce_sum(
                    ST[:, S_SUM + t:S_SUM + t + 1], XR[:, s * 1024:(s + 1) * 1024], AX.X), ve_d)
                wait("scalar", c[0], c[1])
                wait("scalar", sc_d, sq_last[0])
                run("scalar", lambda E, t=t, s=t % 3: E.activation(
                    out=SQ[:], in_=XR[:, s * 1024:(s + 1) * 1024], func=AF.Square,
                    accum_out=ST[:, S_SSQ + t:S_SSQ + t + 1]), sc_d)
                sq_last[0] = sc_d.n
                xr_free[t % 3] = [(ve_d, ve_d.n), (sc_d, sc_d.n)]
                heartbeat()

            wait("vector", sc_d)
            if "fin" not in _skip:
                ln_finalize(16)
            fin1_v = ve_d.n
            if os.environ.get("KSTD"):
                wait("sync", ve_d, ve_d.n)
                wait("sync", sc_d, sc_d.n)
                run("sync", lambda E: E.dma_start(out=out_h[0:128, 0:96], in_=ST[:]), s_out, 16)

            xs_free = [[], []]
            psb_free = [[], []]
            for t in range(NL if "p2" not in _skip else 0):
                c = xload(t, t % 3)
                slot = t % 2
                xslot = t % 3
                for (sem, cnt) in xs_free[slot]:
                    wait("scalar", sem, cnt)
                xs_free[slot] = []
                wait("scalar", c[0], c[1])
                wait("scalar", ve_d, fin1_v)
                run("scalar", lambda E, t=t, s=slot, xs=xslot: E.activation(
                    out=XS[:, s * 1024:(s + 1) * 1024], in_=XR[:, xs * 1024:(xs + 1) * 1024],
                    func=AF.Identity, bias=ST[:, S_NB + t:S_NB + t + 1],
                    scale=ST[:, S_RS + t:S_RS + t + 1]), sc_d)
                xr_free[xslot] = [(sc_d, sc_d.n)]
                wait("tensor", sc_d)
                for (sem, cnt) in psb_free[slot]:
                    wait("tensor", sem, cnt)
                psb_free[slot] = []
                for dc in range(NE):
                    run("tensor", lambda E, s=slot, dc=dc: E.transpose(
                        PSB[:, (6 + s) * 1024 + dc * 128: (6 + s) * 1024 + (dc + 1) * 128],
                        XS[:, s * 1024 + dc * 128: s * 1024 + (dc + 1) * 128], IDT[:]),
                        t_mm if dc == NE - 1 else None)
                xs_free[slot] = [(t_mm, t_mm.n)]
                if os.environ.get("KSCEVAC"):
                    wait("scalar", t_mm)

                    def fev(E, t=t, s=slot):
                        dst = APc(AR[:].tensor, t * 128, [[AR_TOT, 128], [2048, 8], [1, 128]])
                        src = PSB[:, (6 + s) * 1024:(7 + s) * 1024].rearrange("p (c l) -> p c l", l=128)
                        return E.activation(out=dst, in_=src, func=AF.Identity)
                    run("scalar", fev, sc_d)
                    psb_free[slot] = [(sc_d, sc_d.n)]
                    run("vector", lambda E: E.memset(ONES[0:1, 0:1], 1.0), ve_d)
                else:
                    wait("vector", t_mm)

                    def fev(E, t=t, s=slot):
                        dst = APc(AR[:].tensor, t * 128, [[AR_TOT, 128], [2048, 8], [1, 128]])
                        src = PSB[:, (6 + s) * 1024:(7 + s) * 1024].rearrange("p (c l) -> p c l", l=128)
                        return E.tensor_copy(dst, src)
                    run("vector", fev, ve_d)
                    psb_free[slot] = [(ve_d, ve_d.n)]
                    heartbeat()
            xnt_ready = ve_d.n
            st_free[0] = sc_d.n
            if dbg_h is not None:
                wait("sync", ve_d, xnt_ready)
                run("sync", lambda E: E.dma_start(out=dbg_h[:, 0:2048],
                                                  in_=AR[:, 0:2048]), s_dbg, 16)
            if stages <= 1:
                raise _StopBuild()

            def proj_arr1(slot, dst_base, bias_col, sem_w):
                for cc in range(NE):
                    for gq in range(NG):
                        b = (cc * NG + gq) % 4
                        wait("tensor", sem_w[0], sem_w[1])
                        wait("tensor", ve_d, xnt_ready)
                        wait_banks("tensor", {b})
                        for dc in range(NE):
                            run("tensor", lambda E, cc=cc, gq=gq, dc=dc, b=b, sl=slot: E.matmul(
                                PS[:, b * 512:(b + 1) * 512],
                                WA[:, sl * 8192 + dc * 1024 + cc * 128: sl * 8192 + dc * 1024 + (cc + 1) * 128],
                                AR[:, A_XNT + dc * 2048 + gq * 512: A_XNT + dc * 2048 + (gq + 1) * 512],
                                start=(dc == 0), stop=(dc == NE - 1)),
                                t_mm if dc == NE - 1 else None)
                        wait("vector", t_mm)
                        run("vector", lambda E, cc=cc, gq=gq, b=b, dst=dst_base, bc=bias_col:
                            E.tensor_scalar_add(
                                AR[:, dst + cc * 2048 + gq * 512: dst + cc * 2048 + (gq + 1) * 512],
                                PS[:, b * 512:(b + 1) * 512], BQK[:, bc + cc:bc + cc + 1]), ve_d)
                        note_read({b}, ve_d)

            if "proj" not in _skip:
                proj_arr1(0, A_Q, 0, g_wq)
            t_q_done = t_mm.n
            if "proj" not in _skip:
                proj_arr1(1, A_K, 8, g_wk)
            t_k_done = t_mm.n

            if "wvwo" not in _skip:
                wait("gpsimd", t_mm, t_q_done)
                g_wv = load_w1024(wvT_h, 0)
                wait("gpsimd", t_mm, t_k_done)
                g_wo = load_w1024(woT_h, 1)
            else:
                g_wv = g_wo = (gsem[0], 0)

            for t in range(NL if "proj" not in _skip else 0):
                for eg in range(2):
                    b = (t * 2 + eg) % 4
                    wait("tensor", g_wv[0], g_wv[1])
                    wait_banks("tensor", {b})
                    for dc in range(NE):
                        run("tensor", lambda E, t=t, eg=eg, dc=dc, b=b: E.matmul(
                            PS[:, b * 512:(b + 1) * 512],
                            AR[:, A_XNT + dc * 2048 + t * 128: A_XNT + dc * 2048 + (t + 1) * 128],
                            WA[:, dc * 1024 + eg * 512: dc * 1024 + (eg + 1) * 512],
                            start=(dc == 0), stop=(dc == NE - 1)),
                            t_mm if dc == NE - 1 else None)
                    wait("vector", t_mm)

                    def fv(E, t=t, eg=eg, b=b):
                        dst = APc(AR[:].tensor, A_V + t * 1040 + eg * 8 * 65,
                                  [[AR_TOT, 128], [65, 8], [1, 64]])
                        src = PS[:, b * 512:(b + 1) * 512].rearrange("p (h c) -> p h c", c=64)
                        b1 = APc(BV[:].tensor, eg * 512, [[D, 128], [64, 8], [1, 64]])
                        return E.scalar_tensor_tensor(out=dst, in0=src, scalar=1.0, in1=b1,
                                                      op0=ALU.mult, op1=ALU.add)
                    run("vector", fv, ve_d)
                    note_read({b}, ve_d)
            v_all_ready = ve_d.n
            if os.environ.get("KQKV"):
                wait("gpsimd", ve_d, v_all_ready)
                run("gpsimd", lambda E: E.dma_start(out=out_h[0:128, :], in_=AR[:, A_Q:A_Q + 1024]), s_out, 16)
                run("gpsimd", lambda E: E.dma_start(out=out_h[128:256, :], in_=AR[:, A_K:A_K + 1024]), s_out, 16)
                run("gpsimd", lambda E: E.dma_start(out=out_h[256:384, :], in_=AR[:, A_V:A_V + 1024]), s_out, 16)
                run("gpsimd", lambda E: E.dma_start(out=out_h[384:512, :], in_=AR[:, A_XNT:A_XNT + 1024]), s_out, 16)
                wait("gpsimd", s_out, s_out.n)
                run("gpsimd", lambda E: E.dma_start(out=out_h[512:640, :], in_=AR[:, A_Q:A_Q + 1024]), s_out, 16)
                run("gpsimd", lambda E: E.dma_start(out=out_h[640:768, :], in_=AR[:, A_K:A_K + 1024]), s_out, 16)
                run("gpsimd", lambda E: E.dma_start(out=out_h[768:896, :], in_=AR[:, A_V:A_V + 1024]), s_out, 16)
                run("gpsimd", lambda E: E.dma_start(out=out_h[896:1024, :], in_=AR[:, A_XNT:A_XNT + 1024]), s_out, 16)
                run("gpsimd", lambda E: E.dma_start(out=out_h[1024:1152, :], in_=XS[:, 0:1024]), s_out, 16)
                run("gpsimd", lambda E: E.dma_start(out=out_h[1152:1280, :], in_=XS[:, 1024:2048]), s_out, 16)
                wait("sync", ve_d, ve_d.n)
                wait("sync", sc_d, sc_d.n)
                run("sync", lambda E: E.dma_start(out=out_h[1280:1408, 0:96], in_=ST[:]), s_out, 16)
                wait("gpsimd", s_out, s_out.n - 16)
                wait("sync", s_out, s_out.n)
                raise _StopBuild()
            if dbg_h is not None:
                wait("sync", ve_d, v_all_ready)
                run("sync", lambda E: E.dma_start(out=dbg_h[:, 2048:4096],
                                                  in_=AR[:, A_Q:A_Q + 2048]), s_dbg, 16)
                run("sync", lambda E: E.dma_start(out=dbg_h[:, 4096:6144],
                                                  in_=AR[:, A_K:A_K + 2048]), s_dbg, 16)
                run("sync", lambda E: E.dma_start(out=dbg_h[:, 6144:7184],
                                                  in_=AR[:, A_V:A_V + 1040]), s_dbg, 16)
                wait("scalar", s_dbg, s_dbg.n)
            if stages <= 2:
                raise _StopBuild()

            REG = [0, 1536]
            ATA, ATB = 3072, 3584
            reg_guard = {0: [], 1: []}
            at_guard = []
            pb_guard = {0: [], 1: [], 2: [], 3: []}
            astg_free = [[], []]
            trr_free = [[], []]
            or_free = [[], []]
            shift_cnt = 0
            x1_store_cnt = {}
            first_scores = True

            for g in range(NG):
                for p in range(NP):
                    for kb in range(NKB):
                        regA = kb % 2
                        regB = 1 - regA
                        for rg in (regA, regB):
                            for (sem, cnt) in reg_guard[rg]:
                                wait("tensor", sem, cnt)
                            reg_guard[rg] = []
                        if first_scores:
                            wait("tensor", ve_d, v_all_ready)
                            first_scores = False
                        for qc in range(NQC):
                            off = QC_OFF[qc]
                            wdt = QC_W[qc]
                            for (rg, lo, hi, tp) in ((regA, 0, 64, (0, 0)), (regB, 64, 128, (64, 0))):
                                run("tensor", lambda E, p=p, g=g, kb=kb, qc=qc, rg=rg, lo=lo, hi=hi,
                                    tp=tp, off=off, wdt=wdt: E.matmul(
                                        PS[:, REG[rg] + off: REG[rg] + off + wdt],
                                        AR[lo:hi, A_K + p * 2048 + kb * 512 + qc * 128:
                                           A_K + p * 2048 + kb * 512 + (qc + 1) * 128],
                                        AR[lo:hi, A_Q + p * 2048 + g * 512 + qc * 128:
                                           A_Q + p * 2048 + g * 512 + 512],
                                        start=True, stop=True, tile_position=tp),
                                    t_mm if (qc == NQC - 1 and rg == regB) else None)
                        t_scores = t_mm.n
                        for (rg, buf) in ((regA, kb % 2), (regB, 2 + kb % 2)):
                            for (sem, cnt) in pb_guard[buf]:
                                wait("scalar", sem, cnt)
                            pb_guard[buf] = []
                            wait("scalar", t_mm, t_scores)
                            run("scalar", lambda E, rg=rg, buf=buf: E.activation(
                                out=AR[:, A_PB + buf * 1280: A_PB + (buf + 1) * 1280],
                                in_=PS[:, REG[rg]: REG[rg] + 1280], func=AF.Exp, scale=0.125), sc_d)
                            note_read(banks_of(REG[rg], 1280), sc_d)
                            reg_guard[rg].append((sc_d, sc_d.n))
                        sc_exp = sc_d.n
                        wait("vector", sc_d, sc_exp)
                        for buf in (kb % 2, 2 + kb % 2):
                            for (mb, stride) in ((0, 640), (512, 512)):
                                def fm(E, buf=buf, mb=mb, stride=stride):
                                    vv = APc(AR[:].tensor, A_PB + buf * 1280 + mb,
                                             [[AR_TOT, 128], [stride, 2], [1, 128]])
                                    tb = TRI[:].unsqueeze(1).broadcast_to([128, 2, 128])
                                    return E.tensor_tensor(out=vv, in0=vv, in1=tb, op=ALU.mult)
                                run("vector", fm, ve_d if (mb == 512) else None)
                        ve_mask = ve_d.n
                        wait("tensor", ve_d, ve_mask)
                        if kb == 0:
                            for (sem, cnt) in at_guard:
                                wait("tensor", sem, cnt)
                            at_guard = []
                        for (hb, buf, atc) in ((0, kb % 2, ATA), (1, 2 + kb % 2, ATB)):
                            hh = 2 * p + hb
                            for qc in range(NQC):
                                jt = kb * NQC + qc
                                run("tensor", lambda E, hh=hh, jt=jt, qc=qc, buf=buf, atc=atc,
                                    kb=kb: E.matmul(
                                        PS[0:65, atc + qc * 128: atc + 512],
                                        AR[:, A_V + jt * 1040 + hh * 65: A_V + jt * 1040 + hh * 65 + 65],
                                        AR[:, A_PB + buf * 1280 + QC_OFF[qc]:
                                           A_PB + buf * 1280 + QC_OFF[qc] + QC_W[qc]],
                                        start=(kb == 0 and qc == 0),
                                        stop=(kb == NKB - 1 and qc == NQC - 1)),
                                    t_mm if (kb == NKB - 1 and qc == NQC - 1) else None)
                            pb_guard[buf] = [(t_mm, t_mm.n)]
                    t_pv = t_mm.n
                    wait("scalar", t_mm, t_pv)
                    run("scalar", lambda E: E.activation(
                        out=REC[0:1, 0:1024], in_=PS[64:65, ATA:ATA + 1024],
                        func=AF.Ln), sc_d)
                    run("scalar", lambda E: E.activation(
                        out=RECB[0:1, 0:1024], in_=REC[0:1, 0:1024],
                        func=AF.Exp, scale=-1.0), sc_d)
                    sc_rec = sc_d.n
                    wait("tensor", sc_d, sc_rec)
                    for (sem, cnt) in reg_guard[0]:
                        wait("tensor", sem, cnt)
                    reg_guard[0] = []
                    run("tensor", lambda E: E.matmul(PS[0:64, 0:512], ONES[:], RECB[0:1, 0:512],
                        start=True, stop=True), None)
                    run("tensor", lambda E: E.matmul(PS[0:64, 512:1024], ONES[:], RECB[0:1, 512:1024],
                        start=True, stop=True), t_mm)
                    t_r1 = t_mm.n
                    wait("vector", t_mm, t_r1)
                    run("vector", lambda E: E.tensor_copy(RREP[0:64, 0:512], PS[0:64, 0:512]), ve_d)
                    run("vector", lambda E: E.tensor_copy(RREP[0:64, 512:1024], PS[0:64, 512:1024]), ve_d)
                    reg_guard[0].append((ve_d, ve_d.n))
                    run("vector", lambda E, p=p: E.tensor_tensor(
                        out=AR[0:64, A_AS + p * 512: A_AS + (p + 1) * 512],
                        in0=PS[0:64, ATA:ATA + 512], in1=RREP[0:64, 0:512], op=ALU.mult), ve_d)
                    for (sem, cnt) in astg_free[p % 2]:
                        wait("vector", sem, cnt)
                    astg_free[p % 2] = []
                    run("vector", lambda E, p=p: E.tensor_tensor(
                        out=ASTG[0:64, (p % 2) * 512:(p % 2 + 1) * 512],
                        in0=PS[0:64, ATB:ATB + 512], in1=RREP[0:64, 512:1024], op=ALU.mult), ve_d)
                    at_guard = [(ve_d, ve_d.n)]
                    ve_norm = ve_d.n
                    wait("sync", ve_d, ve_norm)
                    run("sync", lambda E, p=p: E.dma_start(
                        out=AR[64:128, A_AS + p * 512: A_AS + (p + 1) * 512],
                        in_=ASTG[0:64, (p % 2) * 512:(p % 2 + 1) * 512]), s_sh, 16)
                    shift_cnt = s_sh.n
                    astg_free[p % 2] = [(s_sh, s_sh.n)]

                wait("tensor", s_sh, shift_cnt)
                wait("tensor", g_wo[0], g_wo[1])
                for rg in (0, 1):
                    for (sem, cnt) in reg_guard[rg]:
                        wait("tensor", sem, cnt)
                    reg_guard[rg] = []
                for (sem, cnt) in at_guard:
                    wait("tensor", sem, cnt)
                at_guard = []
                wait_banks("tensor", set(range(8)))
                for s in range(4):
                    for eg in range(2):
                        b = s * 2 + eg
                        for cc in range(NP):
                            run("tensor", lambda E, s=s, eg=eg, cc=cc, b=b: E.matmul(
                                PS[:, b * 512:(b + 1) * 512],
                                AR[:, A_AS + cc * 512 + s * 128: A_AS + cc * 512 + (s + 1) * 128],
                                WA[:, 8192 + cc * 1024 + eg * 512: 8192 + cc * 1024 + (eg + 1) * 512],
                                start=(cc == 0), stop=(cc == NP - 1)),
                                t_mm if cc == NP - 1 else None)
                        wait("vector", t_mm)

                        def fo(E, s=s, eg=eg, b=b):
                            return E.scalar_tensor_tensor(
                                out=AR[:, A_OS + s * 1024 + eg * 512: A_OS + s * 1024 + (eg + 1) * 512],
                                in0=PS[:, b * 512:(b + 1) * 512], scalar=1.0,
                                in1=BOr[:, eg * 512:(eg + 1) * 512], op0=ALU.mult, op1=ALU.add)
                        run("vector", fo, ve_d)
                        note_read({b}, ve_d)
                ve_oevac = ve_d.n
                reg_guard[0].append((ve_d, ve_oevac))
                reg_guard[1].append((ve_d, ve_oevac))
                at_guard.append((ve_d, ve_oevac))

                wait("vector", ve_d, ve_oevac)
                for s in range(4):
                    run("vector", lambda E, s=s: E.reduce_sum(
                        ST[:, S_SUM + s:S_SUM + s + 1],
                        AR[:, A_OS + s * 1024: A_OS + (s + 1) * 1024], AX.X), ve_d)
                wait("scalar", ve_d, ve_oevac)
                for s in range(4):
                    wait("scalar", sc_d, sq_last[0])
                    run("scalar", lambda E, s=s: E.activation(
                        out=SQ[:], in_=AR[:, A_OS + s * 1024: A_OS + (s + 1) * 1024],
                        func=AF.Square, accum_out=ST[:, S_SSQ + s:S_SSQ + s + 1]), sc_d)
                    sq_last[0] = sc_d.n
                wait("vector", sc_d)
                ln_finalize(4)
                fin2_v = ve_d.n
                for s in range(4):
                    t_ = 4 * g + s
                    c = xload(t_, s % 2)
                    for (sem, cnt) in trr_free[s % 2]:
                        wait("scalar", sem, cnt)
                    trr_free[s % 2] = []
                    wait("scalar", ve_d, fin2_v)
                    run("scalar", lambda E, s=s: E.activation(
                        out=TRR[:, (s % 2) * 1024:((s % 2) + 1) * 1024],
                        in_=AR[:, A_OS + s * 1024: A_OS + (s + 1) * 1024],
                        func=AF.Identity, bias=ST[:, S_NB + s:S_NB + s + 1],
                        scale=ST[:, S_RS + s:S_RS + s + 1]), sc_d)
                    wait("vector", c[0], c[1])
                    wait("vector", sc_d)
                    for (sem, cnt) in or_free[s % 2]:
                        wait("vector", sem, cnt)
                    or_free[s % 2] = []
                    run("vector", lambda E, s=s: E.tensor_tensor(
                        out=ORR[:, (s % 2) * 1024:((s % 2) + 1) * 1024],
                        in0=XR[:, (s % 2) * 1024:((s % 2) + 1) * 1024],
                        in1=TRR[:, (s % 2) * 1024:((s % 2) + 1) * 1024], op=ALU.add), ve_d)
                    xr_free[s % 2] = [(ve_d, ve_d.n)]
                    trr_free[s % 2] = [(ve_d, ve_d.n)]
                    wait("sync", ve_d)
                    run("sync", lambda E, t_=t_, s=s: E.dma_start(
                        out=x1_h[t_ * 128:(t_ + 1) * 128, :],
                        in_=ORR[:, (s % 2) * 1024:((s % 2) + 1) * 1024]), s_x1, 16)
                    x1_store_cnt[t_] = s_x1.n
                    or_free[s % 2] = [(s_x1, s_x1.n)]
                st_free[0] = sc_d.n

            t_attn_done = t_mm.n
            attn_ve_done = ve_d.n
            attn_sc_done = sc_d.n
            if stages <= 3:
                wait("sync", s_x1, s_x1.n)
                run("sync", lambda E: E.dma_start(out=out_h, in_=x1_h), s_out, 16)
                raise _StopBuild()

            wait("gpsimd", t_mm, t_attn_done)
            _fwd = []
            for i6 in range(6):
                def fwd(E, i6=i6):
                    inp = wdT_h.rearrange("(hh p) d -> p hh d", p=128)[:, i6 * 4:(i6 + 1) * 4, :]
                    outp = AR[:, A_WD + i6 * 4096: A_WD + (i6 + 1) * 4096]
                    return E.dma_start(out=outp.rearrange("p (hh d) -> p hh d", d=1024), in_=inp)
                _fwd.append(fwd)
            g_wd = ggroup(_fwd)

            wait("tensor", ve_d, attn_ve_done)
            wait("tensor", sc_d, attn_sc_done)

            wa_guard = {r: [] for r in range(16)}
            wa_init = [(t_mm, t_attn_done)]
            gs_free = [[], []]

            for lg in range(NLG):
                for t in range(8):
                    lt = lg * 8 + t
                    wait("sync", s_x1, x1_store_cnt[lt])
                    c = xload(lt, t % 3, src=x1_h)
                    wait("vector", c[0], c[1])
                    run("vector", lambda E, t=t, s=t % 3: E.reduce_sum(
                        ST[:, S_SUM + t:S_SUM + t + 1], XR[:, s * 1024:(s + 1) * 1024], AX.X), ve_d)
                    wait("scalar", c[0], c[1])
                    wait("scalar", sc_d, sq_last[0])
                    run("scalar", lambda E, t=t, s=t % 3: E.activation(
                        out=SQ[:], in_=XR[:, s * 1024:(s + 1) * 1024], func=AF.Square,
                        accum_out=ST[:, S_SSQ + t:S_SSQ + t + 1]), sc_d)
                    sq_last[0] = sc_d.n
                    xr_free[t % 3] = [(ve_d, ve_d.n), (sc_d, sc_d.n)]
                    heartbeat()
                wait("vector", sc_d)
                ln_finalize(8)
                fin3_v = ve_d.n
                for t in range(8):
                    lt = lg * 8 + t
                    c = xload(lt, t % 3, src=x1_h)
                    slot = t % 2
                    xslot = t % 3
                    for (sem, cnt) in xs_free[slot]:
                        wait("scalar", sem, cnt)
                    xs_free[slot] = []
                    wait("scalar", c[0], c[1])
                    wait("scalar", ve_d, fin3_v)
                    run("scalar", lambda E, t=t, s=slot, xs=xslot: E.activation(
                        out=XS[:, s * 1024:(s + 1) * 1024], in_=XR[:, xs * 1024:(xs + 1) * 1024],
                        func=AF.Identity, bias=ST[:, S_NB + t:S_NB + t + 1],
                        scale=ST[:, S_RS + t:S_RS + t + 1]), sc_d)
                    xr_free[xslot] = [(sc_d, sc_d.n)]
                    wait("tensor", sc_d)
                    for (sem, cnt) in psb_free[slot]:
                        wait("tensor", sem, cnt)
                    psb_free[slot] = []
                    wait_banks("tensor", {slot})
                    for dc in range(NE):
                        run("tensor", lambda E, s=slot, dc=dc: E.transpose(
                            PSB[:, s * 1024 + dc * 128: s * 1024 + (dc + 1) * 128],
                            XS[:, s * 1024 + dc * 128: s * 1024 + (dc + 1) * 128], IDT[:]),
                            t_mm if dc == NE - 1 else None)
                    xs_free[slot] = [(t_mm, t_mm.n)]
                    wait("vector", t_mm)

                    def fev2(E, t=t, s=slot):
                        dst = APc(AR[:].tensor, A_X2T + t * 128, [[AR_TOT, 128], [1024, 8], [1, 128]])
                        src = PSB[:, s * 1024:(s + 1) * 1024].rearrange("p (c l) -> p c l", l=128)
                        return E.tensor_copy(dst, src)
                    run("vector", fev2, ve_d)
                    psb_free[slot] = [(ve_d, ve_d.n)]
                    heartbeat()
                    note_read({slot}, ve_d)
                x2t_ready = ve_d.n
                st_free[0] = sc_d.n
                if os.environ.get("KSUB", "full") == "ln3":
                    raise _StopBuild()

                h_ready = {}
                for hh in range(NHH):
                    rg_ = hh % 8
                    ru_ = 8 + hh % 8
                    for r_ in (rg_, ru_):
                        for (sem, cnt) in (wa_guard[r_] or wa_init):
                            wait("gpsimd", sem, cnt)
                        wa_guard[r_] = []
                    _fsl = []
                    for (src_h, r_) in ((wgT_h, rg_), (wuT_h, ru_)):
                        for half in range(2):
                            def fsl(E, sh=src_h, r=r_, hf=half, hh=hh):
                                inp = sh.rearrange("(dc p) h -> p dc h", p=128)[
                                    :, hf * 4:(hf + 1) * 4, hh * 128:(hh + 1) * 128]
                                outp = WA[:, r * 1024 + hf * 512: r * 1024 + (hf + 1) * 512]
                                return E.dma_start(
                                    out=outp.rearrange("p (dc e) -> p dc e", e=128), in_=inp)
                            _fsl.append(fsl)
                    g_sl = ggroup(_fsl)
                    for l5 in range(2):
                        bg = (hh * 2 + l5) % 2
                        bu = 2 + bg
                        wait("tensor", g_sl[0], g_sl[1])
                        wait("tensor", ve_d, x2t_ready)
                        wait_banks("tensor", {bg, bu})
                        for dc in range(NE):
                            run("tensor", lambda E, dc=dc, bg=bg, r=rg_, l5=l5: E.matmul(
                                PS[:, bg * 512:(bg + 1) * 512],
                                WA[:, r * 1024 + dc * 128: r * 1024 + (dc + 1) * 128],
                                AR[:, A_X2T + dc * 1024 + l5 * 512: A_X2T + dc * 1024 + (l5 + 1) * 512],
                                start=(dc == 0), stop=(dc == NE - 1)),
                                t_mm if dc == NE - 1 else None)
                        t_g = t_mm.n
                        for dc in range(NE):
                            run("tensor", lambda E, dc=dc, bu=bu, r=ru_, l5=l5: E.matmul(
                                PS[:, bu * 512:(bu + 1) * 512],
                                WA[:, r * 1024 + dc * 128: r * 1024 + (dc + 1) * 128],
                                AR[:, A_X2T + dc * 1024 + l5 * 512: A_X2T + dc * 1024 + (l5 + 1) * 512],
                                start=(dc == 0), stop=(dc == NE - 1)),
                                t_mm if dc == NE - 1 else None)
                        t_u = t_mm.n
                        wa_guard[rg_] = [(t_mm, t_g)]
                        wa_guard[ru_] = [(t_mm, t_u)]
                        gslot = (hh * 2 + l5) % 2
                        for (sem, cnt) in gs_free[gslot]:
                            wait("scalar", sem, cnt)
                        gs_free[gslot] = []
                        wait("scalar", t_mm, t_g)
                        run("scalar", lambda E, bg=bg, gs=gslot: E.activation(
                            out=GS[:, gs * 512:(gs + 1) * 512], in_=PS[:, bg * 512:(bg + 1) * 512],
                            func=AF.Silu, scale=FSC), sc_d)
                        note_read({bg}, sc_d)
                        wait("vector", sc_d)
                        wait("vector", t_mm, t_u)
                        run("vector", lambda E, hh=hh, l5=l5, bu=bu, gs=gslot: E.scalar_tensor_tensor(
                            out=AR[:, A_HB + hh * 1024 + l5 * 512: A_HB + hh * 1024 + (l5 + 1) * 512],
                            in0=PS[:, bu * 512:(bu + 1) * 512], scalar=FSC,
                            in1=GS[:, gs * 512:(gs + 1) * 512], op0=ALU.mult, op1=ALU.mult), ve_d)
                        note_read({bu}, ve_d)
                        gs_free[gslot] = [(ve_d, ve_d.n)]
                        h_ready[(hh, l5)] = ve_d.n

                if os.environ.get("KSUB", "full") == "gate":
                    raise _StopBuild()
                wait("tensor", g_wd[0], g_wd[1])
                for l5 in range(2):
                    _order = ([(s, dg, hh) for s in range(4) for dg in range(2)
                               for hh in range(NHH)] if not os.environ.get("KOLDDOWN") else
                              [(s, dg, hh) for hh in range(NHH) for s in range(4)
                               for dg in range(2)])
                    if os.environ.get("KOLDDOWN"):
                        wait_banks("tensor", set(range(8)))
                    for (s, dg, hh) in _order:
                        b = s * 2 + dg
                        if hh == 0 and not os.environ.get("KOLDDOWN"):
                            wait_banks("tensor", {b})
                        if True:
                            if True:
                                wait("tensor", ve_d, h_ready[(hh, l5)])
                                run("tensor", lambda E, hh=hh, s=s, dg=dg, b=b, l5=l5: E.matmul(
                                    PS[:, b * 512:(b + 1) * 512],
                                    AR[:, A_HB + hh * 1024 + l5 * 512 + s * 128:
                                       A_HB + hh * 1024 + l5 * 512 + (s + 1) * 128],
                                    AR[:, A_WD + hh * 1024 + dg * 512: A_WD + hh * 1024 + (dg + 1) * 512],
                                    start=(hh == 0), stop=(hh == NHH - 1)),
                                    t_mm if hh == NHH - 1 else None)
                    t_down = t_mm.n
                    if os.environ.get("KSUB", "full") == "downmm":
                        wait("vector", t_mm, t_down)
                        run("vector", lambda E: E.memset(ST[:, 0:1], 0.0), ve_d)
                        raise _StopBuild()
                    wait("vector", t_mm, t_down)
                    for s in range(4):
                        run("vector", lambda E, s=s: E.reduce_sum(
                            ST[:, S_SUM + s:S_SUM + s + 1], PS[:, s * 1024:(s + 1) * 1024], AX.X), ve_d)
                    if os.environ.get("KSUB", "full") == "red":
                        raise _StopBuild()
                    wait("scalar", t_mm, t_down)
                    wait("scalar", ve_d)
                    for s in range(4):
                        wait("scalar", sc_d, sq_last[0])
                        run("scalar", lambda E, s=s: E.activation(
                            out=SQ[:], in_=PS[:, s * 1024:(s + 1) * 1024], func=AF.Square,
                            scale=FSC, accum_out=ST[:, S_SSQ + s:S_SSQ + s + 1]), sc_d)
                        sq_last[0] = sc_d.n
                    if os.environ.get("KSUB", "full") == "sq":
                        raise _StopBuild()
                    wait("vector", sc_d)
                    ln_finalize(4, f_mean=FSC / 1024.0, f_sq=1.0 / 1024.0, rs_extra=FSC)
                    fin4_v = ve_d.n
                    if os.environ.get("KSUB", "full") == "ln4":
                        raise _StopBuild()
                    for s in range(4):
                        lt = lg * 8 + l5 * 4 + s
                        c = xload(lt, s % 2, src=x1_h)
                        for (sem, cnt) in trr_free[s % 2]:
                            wait("scalar", sem, cnt)
                        trr_free[s % 2] = []
                        wait("scalar", ve_d, fin4_v)
                        run("scalar", lambda E, s=s: E.activation(
                            out=TRR[:, (s % 2) * 1024:((s % 2) + 1) * 1024],
                            in_=PS[:, s * 1024:(s + 1) * 1024],
                            func=AF.Identity, bias=ST[:, S_NB + s:S_NB + s + 1],
                            scale=ST[:, S_RS + s:S_RS + s + 1]), sc_d)
                        note_read({s * 2, s * 2 + 1}, sc_d)
                        wait("vector", c[0], c[1])
                        wait("vector", sc_d)
                        for (sem, cnt) in or_free[s % 2]:
                            wait("vector", sem, cnt)
                        or_free[s % 2] = []
                        run("vector", lambda E, s=s: E.tensor_tensor(
                            out=ORR[:, (s % 2) * 1024:((s % 2) + 1) * 1024],
                            in0=XR[:, (s % 2) * 1024:((s % 2) + 1) * 1024],
                            in1=TRR[:, (s % 2) * 1024:((s % 2) + 1) * 1024], op=ALU.add), ve_d)
                        xr_free[s % 2] = [(ve_d, ve_d.n)]
                        trr_free[s % 2] = [(ve_d, ve_d.n)]
                        wait("sync", ve_d)
                        run("sync", lambda E, lt=lt, s=s: E.dma_start(
                            out=out_h[lt * 128:(lt + 1) * 128, :],
                            in_=ORR[:, (s % 2) * 1024:((s % 2) + 1) * 1024]), s_out, 16)
                        or_free[s % 2] = [(s_out, s_out.n)]
                    st_free[0] = sc_d.n

        except _StopBuild:
            pass
        wait("sync", s_dbg, s_dbg.n)
        wait("sync", s_out, s_out.n)
        wait("sync", s_x1, s_x1.n)
        wait("sync", s_sh, s_sh.n)
        wait("sync", sx[0], sx[0].n)
        wait("sync", sx[1], sx[1].n)
        wait("gpsimd", gsem[0], gsem[0].n)
        wait("gpsimd", gsem[1], gsem[1].n)

        block = ctx.enter_context(nc.Block())

        @block.tensor
        def _(E):
            for f in Qs["tensor"]:
                f(E)

        @block.scalar
        def _(E):
            for f in Qs["scalar"]:
                f(E)

        @block.vector
        def _(E):
            for f in Qs["vector"]:
                f(E)

        @block.gpsimd
        def _(E):
            for f in Qs["gpsimd"]:
                f(E)

        @block.sync
        def _(E):
            for f in Qs["sync"]:
                f(E)

    return nc


_NC_CACHE = None


def _get_nc():
    global _NC_CACHE
    if _NC_CACHE is None:
        _NC_CACHE = build_nc()
    return _NC_CACHE


def make_inputs(**inputs):
    x = np.asarray(inputs["L0_attn_x"], dtype=np.float32)
    wq = np.asarray(inputs["L0_attn_Wq"], dtype=np.float32)
    bq = np.asarray(inputs["L0_attn_bq"], dtype=np.float32)
    wk = np.asarray(inputs["L0_attn_Wk"], dtype=np.float32)
    bk = np.asarray(inputs["L0_attn_bk"], dtype=np.float32)
    wv = np.asarray(inputs["L0_attn_Wv"], dtype=np.float32)
    bv = np.asarray(inputs["L0_attn_bv"], dtype=np.float32)
    wo = np.asarray(inputs["L0_attn_Wo"], dtype=np.float32)
    bo = np.asarray(inputs["L0_attn_bo"], dtype=np.float32)
    wg = np.asarray(inputs["L0_ffn_w_gate"], dtype=np.int32)
    wu = np.asarray(inputs["L0_ffn_w_up"], dtype=np.int32)
    wd = np.asarray(inputs["L0_ffn_w_down"], dtype=np.int32)

    shared = {
        "wqT": np.ascontiguousarray(wq.T),
        "wkT": np.ascontiguousarray(wk.T),
        "wvT": np.ascontiguousarray(wv.T),
        "woT": np.ascontiguousarray(wo.T),
        "bqk": np.ascontiguousarray(
            np.concatenate([bq.reshape(8, 128).T, bk.reshape(8, 128).T], axis=1)),
        "bvr": np.ascontiguousarray(np.broadcast_to(bv[None, :], (128, D))),
        "bor": np.ascontiguousarray(np.broadcast_to(bo[None, :], (128, D))),
        "wgT": np.ascontiguousarray(wg.T),
        "wuT": np.ascontiguousarray(wu.T),
        "wdT": np.ascontiguousarray(wd.T),
        "tri": np.triu(np.ones((128, 128), dtype=np.float32)).astype(ml_dtypes.bfloat16),
        "idt": np.eye(128, dtype=ml_dtypes.bfloat16),
    }
    in_maps = []
    for b in range(8):
        m = dict(shared)
        m["x"] = np.ascontiguousarray(x[b])
        in_maps.append(m)
    return in_maps


def kernel(**inputs):
    in_maps = make_inputs(**inputs)
    nc = _get_nc()
    res = run_bass_kernel_spmd(nc, in_maps, core_ids=list(range(8)))
    out = np.stack([np.asarray(r["out"], dtype=np.float32) for r in res.results], axis=0)
    return out


# version 07
# speedup vs baseline: 1.3452x; 1.1192x over previous
import os
import numpy as np
import ml_dtypes
import concourse.bass as bass
import concourse.mybir as mybir
from concourse.bass_utils import run_bass_kernel_spmd
from concourse.ap import AP as APc
from contextlib import ExitStack

F32 = mybir.dt.float32
BF16 = mybir.dt.bfloat16
I32 = mybir.dt.int32
AF = mybir.ActivationFunctionType
ALU = mybir.AluOpType
AX = mybir.AxisListType

D = 1024
L = 2048
H = 16
DK = 64
HID = 3072
NE = 8
NL = 16
NG = 4
NKB = 4
NQC = 4
NP = 8
NHH = 24
NLG = 2
EPS = 1e-5
FSC = 1.0 / 127.0

AR_TOT = 65792
A_XNT = 0
A_Q = 16384
A_K = 32768
A_V = 49152
A_PB = 0
A_AS = 5120
A_OS = 9216
A_X2T = 0
A_HB = 8192
A_WD = 32768

QC_OFF = {0: 0, 1: 640, 2: 1024, 3: 512}
QC_W = {0: 512, 1: 384, 2: 256, 3: 128}

S_SUM, S_SSQ, S_MEAN, S_VAR, S_RS, S_NB = 0, 16, 32, 48, 64, 80


def build_nc(stages=None):
    if stages is None:
        stages = int(os.environ.get("KSTAGES", "4"))
    nc = bass.Bass("TRN2", target_bir_lowering=False, detect_race_conditions=False)
    x_h = nc.dram_tensor("x", [L, D], F32, kind="ExternalInput").ap()
    wqT_h = nc.dram_tensor("wqT", [D, D], F32, kind="ExternalInput").ap()
    wkT_h = nc.dram_tensor("wkT", [D, D], F32, kind="ExternalInput").ap()
    wvT_h = nc.dram_tensor("wvT", [D, D], F32, kind="ExternalInput").ap()
    woT_h = nc.dram_tensor("woT", [D, D], F32, kind="ExternalInput").ap()
    bqk_h = nc.dram_tensor("bqk", [128, 16], F32, kind="ExternalInput").ap()
    bvr_h = nc.dram_tensor("bvr", [128, D], F32, kind="ExternalInput").ap()
    bor_h = nc.dram_tensor("bor", [128, D], F32, kind="ExternalInput").ap()
    wgT_h = nc.dram_tensor("wgT", [D, HID], I32, kind="ExternalInput").ap()
    wuT_h = nc.dram_tensor("wuT", [D, HID], I32, kind="ExternalInput").ap()
    wdT_h = nc.dram_tensor("wdT", [HID, D], I32, kind="ExternalInput").ap()
    tri_h = nc.dram_tensor("tri", [128, 128], BF16, kind="ExternalInput").ap()
    idt_h = nc.dram_tensor("idt", [128, 128], BF16, kind="ExternalInput").ap()
    x1_kind = "ExternalOutput" if os.environ.get("KDBG") else "Internal"
    x1_h = nc.dram_tensor("x1s", [L, D], F32, kind=x1_kind).ap()
    dbg_h = None
    dbg2_h = None
    if os.environ.get("KDBG"):
        dbg_h = nc.dram_tensor("dbg", [128, 16384], BF16, kind="ExternalOutput").ap()
        dbg2_h = nc.dram_tensor("dbg2", [128, 96], F32, kind="ExternalOutput").ap()
        dbg3_h = nc.dram_tensor("dbg3", [128, 2048], F32, kind="ExternalOutput").ap()
    out_h = nc.dram_tensor("out", [L, D], F32, kind="ExternalOutput").ap()

    ctx = ExitStack()
    with ctx:
        def sb(name, shape, dt=F32):
            return ctx.enter_context(nc.sbuf_tensor(name, shape, dt))

        ST = sb("ST", [128, 96], F32)
        AR = sb("AR", [128, AR_TOT], BF16)
        WA = sb("WA", [128, 16384], BF16)
        XR = sb("XR", [128, 3072], F32)
        ORR = sb("ORR", [128, 2048], F32)
        XS = sb("XS", [128, 2048], BF16)
        TRR = sb("TRR", [128, 2048], BF16)
        GS = sb("GS", [128, 1024], BF16)
        ASTG = sb("ASTG", [64, 1024], BF16)
        RREP = sb("RREP", [64, 1024], BF16)
        BV = sb("BV", [128, D], BF16)
        BOr = sb("BOr", [128, D], BF16)
        BQK = sb("BQK", [128, 16], F32)
        TRI = sb("TRI", [128, 128], BF16)
        IDT = sb("IDT", [128, 128], BF16)
        ONES = sb("ONES", [1, 64], BF16)
        REC = sb("REC", [1, 1024], F32)
        RECB = sb("RECB", [1, 1024], BF16)
        SQ = sb("SQ", [128, 1024], BF16)
        PS = ctx.enter_context(nc.psum_tensor("PS", [128, 4096], F32))
        PSB = PS[:].bitcast(BF16)

        class Sem:
            def __init__(self, name):
                self.name = name
                self.h = ctx.enter_context(nc.semaphore(name))
                self.n = 0

        gsem = [Sem("g0"), Sem("g1"), Sem("g2"), Sem("g3")]
        gprev = [0, 0, 0, 0]
        gpar = [0]
        sx = [Sem("sx0"), Sem("sx1"), Sem("sx2")]
        s_sh = Sem("s_sh")
        s_x1 = Sem("s_x1")
        s_out = Sem("s_out")
        s_dbg = Sem("s_dbg")
        t_mm = Sem("t_mm")
        sc_d = Sem("sc_d")
        ve_d = Sem("ve_d")

        Qs = {k: [] for k in ("tensor", "scalar", "vector", "gpsimd", "sync")}
        waited = {k: {} for k in Qs}

        def emit(eng, fn):
            Qs[eng].append(fn)

        def wait(eng, sem, n=None):
            v = sem.n if n is None else n
            if v <= 0 or waited[eng].get(sem.name, 0) >= v:
                return
            waited[eng][sem.name] = v
            emit(eng, lambda E, h=sem.h, vv=v: E.wait_ge(h, vv))

        own = {"vector": ve_d, "scalar": sc_d}

        def run(eng, fn, inc=None, k=1):
            o = own.get(eng)
            if o is not None:
                wait(eng, o, o.n)
                if inc is None:
                    inc, k = o, 1
            if inc is None:
                emit(eng, fn)
            else:
                emit(eng, lambda E, f=fn, h=inc.h, kk=k: f(E).then_inc(h, kk))
                inc.n += k

        ps_free = {b: [] for b in range(8)}

        def note_read(banks, sem):
            for b in banks:
                ps_free[b].append((sem, sem.n))

        def wait_banks(eng, banks):
            need = {}
            for b in banks:
                for (sem, cnt) in ps_free[b]:
                    need[sem] = max(need.get(sem, 0), cnt)
                ps_free[b] = []
            for sem, cnt in need.items():
                wait(eng, sem, cnt)

        def banks_of(c0, w):
            return set(range(c0 // 512, (c0 + w - 1) // 512 + 1))

        def heartbeat():
            if os.environ.get("KNOHB"):
                return
            wait("tensor", sc_d, sc_d.n)
            run("tensor", lambda E: E.matmul(PS[0:64, 2048:2112], ONES[:], ONES[:],
                                             start=True, stop=True), None)

        def ggroup(fns):
            s = gsem[gpar[0] % 4]
            wait("gpsimd", s, gprev[gpar[0] % 4])
            for f in fns:
                run("gpsimd", f, s, 16)
            gprev[gpar[0] % 4] = s.n
            gpar[0] += 1
            return (s, s.n)

        _skip = os.environ.get("KSKIP", "")
        g_const = ggroup([
            lambda E: E.dma_start(out=TRI[:], in_=tri_h),
            lambda E: E.dma_start(out=IDT[:], in_=idt_h),
            lambda E: E.dma_start(out=BQK[:], in_=bqk_h),
            lambda E: E.dma_start(out=BV[:], in_=bvr_h),
            lambda E: E.dma_start(out=BOr[:], in_=bor_h),
        ]) if "const" not in _skip else (gsem[0], 0)

        def load_w1024(src_h, slot):
            fns = []
            for half in range(2):
                def f(E, s=src_h, sl=slot, hf=half):
                    inp = s.rearrange("(dc p) e -> p dc e", p=128)[:, hf * 4:(hf + 1) * 4, :]
                    outp = WA[:, sl * 8192 + hf * 4096: sl * 8192 + (hf + 1) * 4096]
                    return E.dma_start(out=outp.rearrange("p (dc e) -> p dc e", e=1024), in_=inp)
                fns.append(f)
            return ggroup(fns)

        if "wqk" not in _skip:
            g_wq = load_w1024(wqT_h, 0)
            g_wk = load_w1024(wkT_h, 1)
        else:
            g_wq = g_wk = (gsem[0], 0)

        run("vector", lambda E: E.memset(ONES[:], 1.0), ve_d)

        def f_vones(E):
            v = AR[:, A_V:A_V + 16 * 1040].rearrange("p (t h c) -> p t h c", h=16, c=65)
            return E.memset(v[:, :, :, 64:65], 1.0)
        if "vones" not in _skip:
            run("vector", f_vones, ve_d)

        st_free = [0]
        sq_last = [0]

        xr_free = [[], [], []]

        def xload(tile, slot, src=x_h):
            for (sem, cnt) in xr_free[slot]:
                wait("sync", sem, cnt)
            xr_free[slot] = []
            run("sync", lambda E, t=tile, s=slot, sh=src: E.dma_start(
                out=XR[:, s * 1024:(s + 1) * 1024], in_=sh[t * 128:(t + 1) * 128, :]), sx[slot], 16)
            return (sx[slot], sx[slot].n)

        def ln_finalize(nt, f_mean=1.0 / 1024.0, f_sq=1.0 / 1024.0, rs_extra=None):
            _selfw = os.environ.get("KFINW")
            wait("vector", sc_d, st_free[0])
            run("vector", lambda E, n=nt, f=f_mean: E.tensor_scalar_mul(
                ST[:, S_MEAN:S_MEAN + n], ST[:, S_SUM:S_SUM + n], f), ve_d)
            if _selfw:
                wait("vector", ve_d)
            run("vector", lambda E, n=nt, f=f_sq: E.tensor_scalar(
                ST[:, S_VAR:S_VAR + n], ST[:, S_SSQ:S_SSQ + n], f, EPS, ALU.mult, ALU.add), ve_d)
            if _selfw:
                wait("vector", ve_d)
            run("vector", lambda E, n=nt: E.tensor_tensor(
                ST[:, S_RS:S_RS + n], ST[:, S_MEAN:S_MEAN + n], ST[:, S_MEAN:S_MEAN + n],
                ALU.mult), ve_d)
            if _selfw:
                wait("vector", ve_d)
            run("vector", lambda E, n=nt: E.tensor_tensor(
                ST[:, S_VAR:S_VAR + n], ST[:, S_VAR:S_VAR + n], ST[:, S_RS:S_RS + n],
                ALU.subtract), ve_d)
            if os.environ.get("KFIN") and nt == 16:
                wait("sync", ve_d, ve_d.n)
                run("sync", lambda E: E.dma_start(out=out_h[1408:1536, 0:96], in_=ST[:]), s_out, 16)
                wait("scalar", s_out, s_out.n)
            wait("scalar", ve_d)
            run("scalar", lambda E, n=nt: E.activation(
                out=ST[:, S_NB:S_NB + n], in_=ST[:, S_VAR:S_VAR + n], func=AF.Ln), sc_d)
            run("scalar", lambda E, n=nt: E.activation(
                out=ST[:, S_RS:S_RS + n], in_=ST[:, S_NB:S_NB + n], func=AF.Exp,
                scale=-0.5), sc_d)
            wait("vector", sc_d)
            run("vector", lambda E, n=nt: E.tensor_tensor(
                ST[:, S_NB:S_NB + n], ST[:, S_MEAN:S_MEAN + n], ST[:, S_RS:S_RS + n],
                ALU.mult), ve_d)
            run("vector", lambda E, n=nt: E.tensor_scalar_mul(
                ST[:, S_NB:S_NB + n], ST[:, S_NB:S_NB + n], -1.0), ve_d)
            if rs_extra is not None:
                run("vector", lambda E, n=nt, r=rs_extra: E.tensor_scalar_mul(
                    ST[:, S_RS:S_RS + n], ST[:, S_RS:S_RS + n], r), ve_d)

        class _StopBuild(Exception):
            pass
        try:
            for t in range(NL):
                c = xload(t, t % 3)
                if dbg2_h is not None and t == 1 and os.environ.get("KDBG2"):
                    wait("sync", sx[0], sx[0].n)
                    wait("sync", sx[1], sx[1].n)
                    run("sync", lambda E: E.dma_start(out=dbg3_h, in_=XR[:]), s_dbg, 16)
                if os.environ.get("KDBG2") and t >= 2:
                    wait("sync", s_dbg, 16)
                wait("vector", c[0], c[1])
                run("vector", lambda E, t=t, s=t % 3: E.reduce_sum(
                    ST[:, S_SUM + t:S_SUM + t + 1], XR[:, s * 1024:(s + 1) * 1024], AX.X), ve_d)
                wait("scalar", c[0], c[1])
                wait("scalar", sc_d, sq_last[0])
                run("scalar", lambda E, t=t, s=t % 3: E.activation(
                    out=SQ[:], in_=XR[:, s * 1024:(s + 1) * 1024], func=AF.Square,
                    accum_out=ST[:, S_SSQ + t:S_SSQ + t + 1]), sc_d)
                sq_last[0] = sc_d.n
                xr_free[t % 3] = [(ve_d, ve_d.n), (sc_d, sc_d.n)]
                heartbeat()

            wait("vector", sc_d)
            if "fin" not in _skip:
                ln_finalize(16)
            fin1_v = ve_d.n
            if os.environ.get("KSTD"):
                wait("sync", ve_d, ve_d.n)
                wait("sync", sc_d, sc_d.n)
                run("sync", lambda E: E.dma_start(out=out_h[0:128, 0:96], in_=ST[:]), s_out, 16)

            xs_free = [[], []]
            psb_free = [[], []]
            for t in range(NL if "p2" not in _skip else 0):
                c = xload(t, t % 3)
                slot = t % 2
                xslot = t % 3
                for (sem, cnt) in xs_free[slot]:
                    wait("scalar", sem, cnt)
                xs_free[slot] = []
                wait("scalar", c[0], c[1])
                wait("scalar", ve_d, fin1_v)
                run("scalar", lambda E, t=t, s=slot, xs=xslot: E.activation(
                    out=XS[:, s * 1024:(s + 1) * 1024], in_=XR[:, xs * 1024:(xs + 1) * 1024],
                    func=AF.Identity, bias=ST[:, S_NB + t:S_NB + t + 1],
                    scale=ST[:, S_RS + t:S_RS + t + 1]), sc_d)
                xr_free[xslot] = [(sc_d, sc_d.n)]
                wait("tensor", sc_d)
                for (sem, cnt) in psb_free[slot]:
                    wait("tensor", sem, cnt)
                psb_free[slot] = []
                for dc in range(NE):
                    run("tensor", lambda E, s=slot, dc=dc: E.transpose(
                        PSB[:, (6 + s) * 1024 + dc * 128: (6 + s) * 1024 + (dc + 1) * 128],
                        XS[:, s * 1024 + dc * 128: s * 1024 + (dc + 1) * 128], IDT[:]),
                        t_mm if dc == NE - 1 else None)
                xs_free[slot] = [(t_mm, t_mm.n)]
                if os.environ.get("KSCEVAC"):
                    wait("scalar", t_mm)

                    def fev(E, t=t, s=slot):
                        dst = APc(AR[:].tensor, t * 128, [[AR_TOT, 128], [2048, 8], [1, 128]])
                        src = PSB[:, (6 + s) * 1024:(7 + s) * 1024].rearrange("p (c l) -> p c l", l=128)
                        return E.activation(out=dst, in_=src, func=AF.Identity)
                    run("scalar", fev, sc_d)
                    psb_free[slot] = [(sc_d, sc_d.n)]
                    run("vector", lambda E: E.memset(ONES[0:1, 0:1], 1.0), ve_d)
                else:
                    wait("vector", t_mm)

                    def fev(E, t=t, s=slot):
                        dst = APc(AR[:].tensor, t * 128, [[AR_TOT, 128], [2048, 8], [1, 128]])
                        src = PSB[:, (6 + s) * 1024:(7 + s) * 1024].rearrange("p (c l) -> p c l", l=128)
                        return E.tensor_copy(dst, src)
                    run("vector", fev, ve_d)
                    psb_free[slot] = [(ve_d, ve_d.n)]
                    heartbeat()
            xnt_ready = ve_d.n
            st_free[0] = sc_d.n
            if dbg_h is not None:
                wait("sync", ve_d, xnt_ready)
                run("sync", lambda E: E.dma_start(out=dbg_h[:, 0:2048],
                                                  in_=AR[:, 0:2048]), s_dbg, 16)
            if stages <= 1:
                raise _StopBuild()

            def proj_arr1(slot, dst_base, bias_col, sem_w):
                for cc in range(NE):
                    for gq in range(NG):
                        b = (cc * NG + gq) % 4
                        wait("tensor", sem_w[0], sem_w[1])
                        wait("tensor", ve_d, xnt_ready)
                        wait_banks("tensor", {b})
                        for dc in range(NE):
                            run("tensor", lambda E, cc=cc, gq=gq, dc=dc, b=b, sl=slot: E.matmul(
                                PS[:, b * 512:(b + 1) * 512],
                                WA[:, sl * 8192 + dc * 1024 + cc * 128: sl * 8192 + dc * 1024 + (cc + 1) * 128],
                                AR[:, A_XNT + dc * 2048 + gq * 512: A_XNT + dc * 2048 + (gq + 1) * 512],
                                start=(dc == 0), stop=(dc == NE - 1)),
                                t_mm if dc == NE - 1 else None)
                        wait("vector", t_mm)
                        run("vector", lambda E, cc=cc, gq=gq, b=b, dst=dst_base, bc=bias_col:
                            E.tensor_scalar_add(
                                AR[:, dst + cc * 2048 + gq * 512: dst + cc * 2048 + (gq + 1) * 512],
                                PS[:, b * 512:(b + 1) * 512], BQK[:, bc + cc:bc + cc + 1]), ve_d)
                        note_read({b}, ve_d)

            if "proj" not in _skip:
                proj_arr1(0, A_Q, 0, g_wq)
            t_q_done = t_mm.n
            if "proj" not in _skip:
                proj_arr1(1, A_K, 8, g_wk)
            t_k_done = t_mm.n

            if "wvwo" not in _skip:
                wait("gpsimd", t_mm, t_q_done)
                g_wv = load_w1024(wvT_h, 0)
                wait("gpsimd", t_mm, t_k_done)
                g_wo = load_w1024(woT_h, 1)
            else:
                g_wv = g_wo = (gsem[0], 0)

            for t in range(NL if "proj" not in _skip else 0):
                for eg in range(2):
                    b = (t * 2 + eg) % 4
                    wait("tensor", g_wv[0], g_wv[1])
                    wait_banks("tensor", {b})
                    for dc in range(NE):
                        run("tensor", lambda E, t=t, eg=eg, dc=dc, b=b: E.matmul(
                            PS[:, b * 512:(b + 1) * 512],
                            AR[:, A_XNT + dc * 2048 + t * 128: A_XNT + dc * 2048 + (t + 1) * 128],
                            WA[:, dc * 1024 + eg * 512: dc * 1024 + (eg + 1) * 512],
                            start=(dc == 0), stop=(dc == NE - 1)),
                            t_mm if dc == NE - 1 else None)
                    wait("vector", t_mm)

                    def fv(E, t=t, eg=eg, b=b):
                        dst = APc(AR[:].tensor, A_V + t * 1040 + eg * 8 * 65,
                                  [[AR_TOT, 128], [65, 8], [1, 64]])
                        src = PS[:, b * 512:(b + 1) * 512].rearrange("p (h c) -> p h c", c=64)
                        b1 = APc(BV[:].tensor, eg * 512, [[D, 128], [64, 8], [1, 64]])
                        return E.scalar_tensor_tensor(out=dst, in0=src, scalar=1.0, in1=b1,
                                                      op0=ALU.mult, op1=ALU.add)
                    run("vector", fv, ve_d)
                    note_read({b}, ve_d)
            v_all_ready = ve_d.n
            if os.environ.get("KQKV"):
                wait("gpsimd", ve_d, v_all_ready)
                run("gpsimd", lambda E: E.dma_start(out=out_h[0:128, :], in_=AR[:, A_Q:A_Q + 1024]), s_out, 16)
                run("gpsimd", lambda E: E.dma_start(out=out_h[128:256, :], in_=AR[:, A_K:A_K + 1024]), s_out, 16)
                run("gpsimd", lambda E: E.dma_start(out=out_h[256:384, :], in_=AR[:, A_V:A_V + 1024]), s_out, 16)
                run("gpsimd", lambda E: E.dma_start(out=out_h[384:512, :], in_=AR[:, A_XNT:A_XNT + 1024]), s_out, 16)
                wait("gpsimd", s_out, s_out.n)
                run("gpsimd", lambda E: E.dma_start(out=out_h[512:640, :], in_=AR[:, A_Q:A_Q + 1024]), s_out, 16)
                run("gpsimd", lambda E: E.dma_start(out=out_h[640:768, :], in_=AR[:, A_K:A_K + 1024]), s_out, 16)
                run("gpsimd", lambda E: E.dma_start(out=out_h[768:896, :], in_=AR[:, A_V:A_V + 1024]), s_out, 16)
                run("gpsimd", lambda E: E.dma_start(out=out_h[896:1024, :], in_=AR[:, A_XNT:A_XNT + 1024]), s_out, 16)
                run("gpsimd", lambda E: E.dma_start(out=out_h[1024:1152, :], in_=XS[:, 0:1024]), s_out, 16)
                run("gpsimd", lambda E: E.dma_start(out=out_h[1152:1280, :], in_=XS[:, 1024:2048]), s_out, 16)
                wait("sync", ve_d, ve_d.n)
                wait("sync", sc_d, sc_d.n)
                run("sync", lambda E: E.dma_start(out=out_h[1280:1408, 0:96], in_=ST[:]), s_out, 16)
                wait("gpsimd", s_out, s_out.n - 16)
                wait("sync", s_out, s_out.n)
                raise _StopBuild()
            if dbg_h is not None:
                wait("sync", ve_d, v_all_ready)
                run("sync", lambda E: E.dma_start(out=dbg_h[:, 2048:4096],
                                                  in_=AR[:, A_Q:A_Q + 2048]), s_dbg, 16)
                run("sync", lambda E: E.dma_start(out=dbg_h[:, 4096:6144],
                                                  in_=AR[:, A_K:A_K + 2048]), s_dbg, 16)
                run("sync", lambda E: E.dma_start(out=dbg_h[:, 6144:7184],
                                                  in_=AR[:, A_V:A_V + 1040]), s_dbg, 16)
                wait("scalar", s_dbg, s_dbg.n)
            if stages <= 2:
                raise _StopBuild()

            REG = [0, 1536]
            ATA, ATB = 3072, 3584
            reg_guard = {0: [], 1: []}
            at_guard = []
            pb_guard = {0: [], 1: [], 2: [], 3: []}
            astg_free = [[], []]
            trr_free = [[], []]
            or_free = [[], []]
            shift_cnt = 0
            x1_store_cnt = {}
            first_scores = True

            for g in range(NG):
                for p in range(NP):
                    for kb in range(NKB):
                        regA = kb % 2
                        regB = 1 - regA
                        for rg in (regA, regB):
                            for (sem, cnt) in reg_guard[rg]:
                                wait("tensor", sem, cnt)
                            reg_guard[rg] = []
                        if first_scores:
                            wait("tensor", ve_d, v_all_ready)
                            first_scores = False
                        for qc in range(NQC):
                            off = QC_OFF[qc]
                            wdt = QC_W[qc]
                            for (rg, lo, hi, tp) in ((regA, 0, 64, (0, 0)), (regB, 64, 128, (64, 0))):
                                run("tensor", lambda E, p=p, g=g, kb=kb, qc=qc, rg=rg, lo=lo, hi=hi,
                                    tp=tp, off=off, wdt=wdt: E.matmul(
                                        PS[:, REG[rg] + off: REG[rg] + off + wdt],
                                        AR[lo:hi, A_K + p * 2048 + kb * 512 + qc * 128:
                                           A_K + p * 2048 + kb * 512 + (qc + 1) * 128],
                                        AR[lo:hi, A_Q + p * 2048 + g * 512 + qc * 128:
                                           A_Q + p * 2048 + g * 512 + 512],
                                        start=True, stop=True, tile_position=tp),
                                    t_mm if (qc == NQC - 1 and rg == regB) else None)
                        t_scores = t_mm.n
                        ve_mask_h = {}
                        for (rg, buf) in ((regA, kb % 2), (regB, 2 + kb % 2)):
                            for (sem, cnt) in pb_guard[buf]:
                                wait("scalar", sem, cnt)
                            pb_guard[buf] = []
                            wait("scalar", t_mm, t_scores)
                            run("scalar", lambda E, rg=rg, buf=buf: E.activation(
                                out=AR[:, A_PB + buf * 1280: A_PB + (buf + 1) * 1280],
                                in_=PS[:, REG[rg]: REG[rg] + 1280], func=AF.Exp, scale=0.125), sc_d)
                            note_read(banks_of(REG[rg], 1280), sc_d)
                            reg_guard[rg].append((sc_d, sc_d.n))
                            sc_exp_h = sc_d.n
                            wait("vector", sc_d, sc_exp_h)
                            for (mb, stride) in ((0, 640), (512, 512)):
                                def fm(E, buf=buf, mb=mb, stride=stride):
                                    vv = APc(AR[:].tensor, A_PB + buf * 1280 + mb,
                                             [[AR_TOT, 128], [stride, 2], [1, 128]])
                                    tb = TRI[:].unsqueeze(1).broadcast_to([128, 2, 128])
                                    return E.tensor_tensor(out=vv, in0=vv, in1=tb, op=ALU.mult)
                                run("vector", fm, ve_d)
                            ve_mask_h[buf] = ve_d.n
                        if kb == 0:
                            for (sem, cnt) in at_guard:
                                wait("tensor", sem, cnt)
                            at_guard = []
                        for (hb, buf, atc) in ((0, kb % 2, ATA), (1, 2 + kb % 2, ATB)):
                            wait("tensor", ve_d, ve_mask_h[buf])
                            hh = 2 * p + hb
                            for qc in range(NQC):
                                jt = kb * NQC + qc
                                run("tensor", lambda E, hh=hh, jt=jt, qc=qc, buf=buf, atc=atc,
                                    kb=kb: E.matmul(
                                        PS[0:65, atc + qc * 128: atc + 512],
                                        AR[:, A_V + jt * 1040 + hh * 65: A_V + jt * 1040 + hh * 65 + 65],
                                        AR[:, A_PB + buf * 1280 + QC_OFF[qc]:
                                           A_PB + buf * 1280 + QC_OFF[qc] + QC_W[qc]],
                                        start=(kb == 0 and qc == 0),
                                        stop=(kb == NKB - 1 and qc == NQC - 1)),
                                    t_mm if (kb == NKB - 1 and qc == NQC - 1) else None)
                            pb_guard[buf] = [(t_mm, t_mm.n)]
                    t_pv = t_mm.n
                    wait("scalar", t_mm, t_pv)
                    run("scalar", lambda E: E.activation(
                        out=REC[0:1, 0:1024], in_=PS[64:65, ATA:ATA + 1024],
                        func=AF.Ln), sc_d)
                    run("scalar", lambda E: E.activation(
                        out=RECB[0:1, 0:1024], in_=REC[0:1, 0:1024],
                        func=AF.Exp, scale=-1.0), sc_d)
                    sc_rec = sc_d.n
                    wait("tensor", sc_d, sc_rec)
                    for (sem, cnt) in reg_guard[0]:
                        wait("tensor", sem, cnt)
                    reg_guard[0] = []
                    run("tensor", lambda E: E.matmul(PS[0:64, 0:512], ONES[:], RECB[0:1, 0:512],
                        start=True, stop=True), None)
                    run("tensor", lambda E: E.matmul(PS[0:64, 512:1024], ONES[:], RECB[0:1, 512:1024],
                        start=True, stop=True), t_mm)
                    t_r1 = t_mm.n
                    wait("vector", t_mm, t_r1)
                    run("vector", lambda E: E.tensor_copy(RREP[0:64, 0:512], PS[0:64, 0:512]), ve_d)
                    run("vector", lambda E: E.tensor_copy(RREP[0:64, 512:1024], PS[0:64, 512:1024]), ve_d)
                    reg_guard[0].append((ve_d, ve_d.n))
                    run("vector", lambda E, p=p: E.tensor_tensor(
                        out=AR[0:64, A_AS + p * 512: A_AS + (p + 1) * 512],
                        in0=PS[0:64, ATA:ATA + 512], in1=RREP[0:64, 0:512], op=ALU.mult), ve_d)
                    for (sem, cnt) in astg_free[p % 2]:
                        wait("vector", sem, cnt)
                    astg_free[p % 2] = []
                    run("vector", lambda E, p=p: E.tensor_tensor(
                        out=ASTG[0:64, (p % 2) * 512:(p % 2 + 1) * 512],
                        in0=PS[0:64, ATB:ATB + 512], in1=RREP[0:64, 512:1024], op=ALU.mult), ve_d)
                    at_guard = [(ve_d, ve_d.n)]
                    ve_norm = ve_d.n
                    wait("sync", ve_d, ve_norm)
                    run("sync", lambda E, p=p: E.dma_start(
                        out=AR[64:128, A_AS + p * 512: A_AS + (p + 1) * 512],
                        in_=ASTG[0:64, (p % 2) * 512:(p % 2 + 1) * 512]), s_sh, 16)
                    shift_cnt = s_sh.n
                    astg_free[p % 2] = [(s_sh, s_sh.n)]

                wait("tensor", s_sh, shift_cnt)
                wait("tensor", g_wo[0], g_wo[1])
                for rg in (0, 1):
                    for (sem, cnt) in reg_guard[rg]:
                        wait("tensor", sem, cnt)
                    reg_guard[rg] = []
                for (sem, cnt) in at_guard:
                    wait("tensor", sem, cnt)
                at_guard = []
                wait_banks("tensor", set(range(8)))
                for s in range(4):
                    for eg in range(2):
                        b = s * 2 + eg
                        for cc in range(NP):
                            run("tensor", lambda E, s=s, eg=eg, cc=cc, b=b: E.matmul(
                                PS[:, b * 512:(b + 1) * 512],
                                AR[:, A_AS + cc * 512 + s * 128: A_AS + cc * 512 + (s + 1) * 128],
                                WA[:, 8192 + cc * 1024 + eg * 512: 8192 + cc * 1024 + (eg + 1) * 512],
                                start=(cc == 0), stop=(cc == NP - 1)),
                                t_mm if cc == NP - 1 else None)
                        wait("vector", t_mm)

                        def fo(E, s=s, eg=eg, b=b):
                            return E.scalar_tensor_tensor(
                                out=AR[:, A_OS + s * 1024 + eg * 512: A_OS + s * 1024 + (eg + 1) * 512],
                                in0=PS[:, b * 512:(b + 1) * 512], scalar=1.0,
                                in1=BOr[:, eg * 512:(eg + 1) * 512], op0=ALU.mult, op1=ALU.add)
                        run("vector", fo, ve_d)
                        note_read({b}, ve_d)
                ve_oevac = ve_d.n
                reg_guard[0].append((ve_d, ve_oevac))
                reg_guard[1].append((ve_d, ve_oevac))
                at_guard.append((ve_d, ve_oevac))

                wait("vector", ve_d, ve_oevac)
                for s in range(4):
                    run("vector", lambda E, s=s: E.reduce_sum(
                        ST[:, S_SUM + s:S_SUM + s + 1],
                        AR[:, A_OS + s * 1024: A_OS + (s + 1) * 1024], AX.X), ve_d)
                wait("scalar", ve_d, ve_oevac)
                for s in range(4):
                    wait("scalar", sc_d, sq_last[0])
                    run("scalar", lambda E, s=s: E.activation(
                        out=SQ[:], in_=AR[:, A_OS + s * 1024: A_OS + (s + 1) * 1024],
                        func=AF.Square, accum_out=ST[:, S_SSQ + s:S_SSQ + s + 1]), sc_d)
                    sq_last[0] = sc_d.n
                wait("vector", sc_d)
                ln_finalize(4)
                fin2_v = ve_d.n
                for s in range(4):
                    t_ = 4 * g + s
                    c = xload(t_, s % 2)
                    for (sem, cnt) in trr_free[s % 2]:
                        wait("scalar", sem, cnt)
                    trr_free[s % 2] = []
                    wait("scalar", ve_d, fin2_v)
                    run("scalar", lambda E, s=s: E.activation(
                        out=TRR[:, (s % 2) * 1024:((s % 2) + 1) * 1024],
                        in_=AR[:, A_OS + s * 1024: A_OS + (s + 1) * 1024],
                        func=AF.Identity, bias=ST[:, S_NB + s:S_NB + s + 1],
                        scale=ST[:, S_RS + s:S_RS + s + 1]), sc_d)
                    wait("vector", c[0], c[1])
                    wait("vector", sc_d)
                    for (sem, cnt) in or_free[s % 2]:
                        wait("vector", sem, cnt)
                    or_free[s % 2] = []
                    run("vector", lambda E, s=s: E.tensor_tensor(
                        out=ORR[:, (s % 2) * 1024:((s % 2) + 1) * 1024],
                        in0=XR[:, (s % 2) * 1024:((s % 2) + 1) * 1024],
                        in1=TRR[:, (s % 2) * 1024:((s % 2) + 1) * 1024], op=ALU.add), ve_d)
                    xr_free[s % 2] = [(ve_d, ve_d.n)]
                    trr_free[s % 2] = [(ve_d, ve_d.n)]
                    wait("sync", ve_d)
                    run("sync", lambda E, t_=t_, s=s: E.dma_start(
                        out=x1_h[t_ * 128:(t_ + 1) * 128, :],
                        in_=ORR[:, (s % 2) * 1024:((s % 2) + 1) * 1024]), s_x1, 16)
                    x1_store_cnt[t_] = s_x1.n
                    or_free[s % 2] = [(s_x1, s_x1.n)]
                st_free[0] = sc_d.n

            t_attn_done = t_mm.n
            attn_ve_done = ve_d.n
            attn_sc_done = sc_d.n
            if stages <= 3:
                wait("sync", s_x1, s_x1.n)
                run("sync", lambda E: E.dma_start(out=out_h, in_=x1_h), s_out, 16)
                raise _StopBuild()

            wait("gpsimd", t_mm, t_attn_done)
            _fwd = []
            for i6 in range(6):
                def fwd(E, i6=i6):
                    inp = wdT_h.rearrange("(hh p) d -> p hh d", p=128)[:, i6 * 4:(i6 + 1) * 4, :]
                    outp = AR[:, A_WD + i6 * 4096: A_WD + (i6 + 1) * 4096]
                    return E.dma_start(out=outp.rearrange("p (hh d) -> p hh d", d=1024), in_=inp)
                _fwd.append(fwd)
            g_wd = ggroup(_fwd)

            wait("tensor", ve_d, attn_ve_done)
            wait("tensor", sc_d, attn_sc_done)

            wa_guard = {r: [] for r in range(16)}
            wa_init = [(t_mm, t_attn_done)]
            gs_free = [[], []]

            for lg in range(NLG):
                for t in range(8):
                    lt = lg * 8 + t
                    wait("sync", s_x1, x1_store_cnt[lt])
                    c = xload(lt, t % 3, src=x1_h)
                    wait("vector", c[0], c[1])
                    run("vector", lambda E, t=t, s=t % 3: E.reduce_sum(
                        ST[:, S_SUM + t:S_SUM + t + 1], XR[:, s * 1024:(s + 1) * 1024], AX.X), ve_d)
                    wait("scalar", c[0], c[1])
                    wait("scalar", sc_d, sq_last[0])
                    run("scalar", lambda E, t=t, s=t % 3: E.activation(
                        out=SQ[:], in_=XR[:, s * 1024:(s + 1) * 1024], func=AF.Square,
                        accum_out=ST[:, S_SSQ + t:S_SSQ + t + 1]), sc_d)
                    sq_last[0] = sc_d.n
                    xr_free[t % 3] = [(ve_d, ve_d.n), (sc_d, sc_d.n)]
                    heartbeat()
                wait("vector", sc_d)
                ln_finalize(8)
                fin3_v = ve_d.n
                for t in range(8):
                    lt = lg * 8 + t
                    c = xload(lt, t % 3, src=x1_h)
                    slot = t % 2
                    xslot = t % 3
                    for (sem, cnt) in xs_free[slot]:
                        wait("scalar", sem, cnt)
                    xs_free[slot] = []
                    wait("scalar", c[0], c[1])
                    wait("scalar", ve_d, fin3_v)
                    run("scalar", lambda E, t=t, s=slot, xs=xslot: E.activation(
                        out=XS[:, s * 1024:(s + 1) * 1024], in_=XR[:, xs * 1024:(xs + 1) * 1024],
                        func=AF.Identity, bias=ST[:, S_NB + t:S_NB + t + 1],
                        scale=ST[:, S_RS + t:S_RS + t + 1]), sc_d)
                    xr_free[xslot] = [(sc_d, sc_d.n)]
                    wait("tensor", sc_d)
                    for (sem, cnt) in psb_free[slot]:
                        wait("tensor", sem, cnt)
                    psb_free[slot] = []
                    wait_banks("tensor", {slot})
                    for dc in range(NE):
                        run("tensor", lambda E, s=slot, dc=dc: E.transpose(
                            PSB[:, s * 1024 + dc * 128: s * 1024 + (dc + 1) * 128],
                            XS[:, s * 1024 + dc * 128: s * 1024 + (dc + 1) * 128], IDT[:]),
                            t_mm if dc == NE - 1 else None)
                    xs_free[slot] = [(t_mm, t_mm.n)]
                    wait("vector", t_mm)

                    def fev2(E, t=t, s=slot):
                        dst = APc(AR[:].tensor, A_X2T + t * 128, [[AR_TOT, 128], [1024, 8], [1, 128]])
                        src = PSB[:, s * 1024:(s + 1) * 1024].rearrange("p (c l) -> p c l", l=128)
                        return E.tensor_copy(dst, src)
                    run("vector", fev2, ve_d)
                    psb_free[slot] = [(ve_d, ve_d.n)]
                    heartbeat()
                    note_read({slot}, ve_d)
                x2t_ready = ve_d.n
                st_free[0] = sc_d.n
                if os.environ.get("KSUB", "full") == "ln3":
                    raise _StopBuild()

                h_ready = {}
                for hh in range(NHH):
                    rg_ = hh % 8
                    ru_ = 8 + hh % 8
                    for r_ in (rg_, ru_):
                        for (sem, cnt) in (wa_guard[r_] or wa_init):
                            wait("gpsimd", sem, cnt)
                        wa_guard[r_] = []
                    _fsl = []
                    for (src_h, r_) in ((wgT_h, rg_), (wuT_h, ru_)):
                        for half in range(2):
                            def fsl(E, sh=src_h, r=r_, hf=half, hh=hh):
                                inp = sh.rearrange("(dc p) h -> p dc h", p=128)[
                                    :, hf * 4:(hf + 1) * 4, hh * 128:(hh + 1) * 128]
                                outp = WA[:, r * 1024 + hf * 512: r * 1024 + (hf + 1) * 512]
                                return E.dma_start(
                                    out=outp.rearrange("p (dc e) -> p dc e", e=128), in_=inp)
                            _fsl.append(fsl)
                    g_sl = ggroup(_fsl)
                    for l5 in range(2):
                        bg = (hh * 2 + l5) % 2
                        bu = 2 + bg
                        wait("tensor", g_sl[0], g_sl[1])
                        wait("tensor", ve_d, x2t_ready)
                        wait_banks("tensor", {bg, bu})
                        for dc in range(NE):
                            run("tensor", lambda E, dc=dc, bg=bg, r=rg_, l5=l5: E.matmul(
                                PS[:, bg * 512:(bg + 1) * 512],
                                WA[:, r * 1024 + dc * 128: r * 1024 + (dc + 1) * 128],
                                AR[:, A_X2T + dc * 1024 + l5 * 512: A_X2T + dc * 1024 + (l5 + 1) * 512],
                                start=(dc == 0), stop=(dc == NE - 1)),
                                t_mm if dc == NE - 1 else None)
                        t_g = t_mm.n
                        for dc in range(NE):
                            run("tensor", lambda E, dc=dc, bu=bu, r=ru_, l5=l5: E.matmul(
                                PS[:, bu * 512:(bu + 1) * 512],
                                WA[:, r * 1024 + dc * 128: r * 1024 + (dc + 1) * 128],
                                AR[:, A_X2T + dc * 1024 + l5 * 512: A_X2T + dc * 1024 + (l5 + 1) * 512],
                                start=(dc == 0), stop=(dc == NE - 1)),
                                t_mm if dc == NE - 1 else None)
                        t_u = t_mm.n
                        wa_guard[rg_] = [(t_mm, t_g)]
                        wa_guard[ru_] = [(t_mm, t_u)]
                        gslot = (hh * 2 + l5) % 2
                        for (sem, cnt) in gs_free[gslot]:
                            wait("scalar", sem, cnt)
                        gs_free[gslot] = []
                        wait("scalar", t_mm, t_g)
                        run("scalar", lambda E, bg=bg, gs=gslot: E.activation(
                            out=GS[:, gs * 512:(gs + 1) * 512], in_=PS[:, bg * 512:(bg + 1) * 512],
                            func=AF.Silu, scale=FSC), sc_d)
                        note_read({bg}, sc_d)
                        wait("vector", sc_d)
                        wait("vector", t_mm, t_u)
                        run("vector", lambda E, hh=hh, l5=l5, bu=bu, gs=gslot: E.scalar_tensor_tensor(
                            out=AR[:, A_HB + hh * 1024 + l5 * 512: A_HB + hh * 1024 + (l5 + 1) * 512],
                            in0=PS[:, bu * 512:(bu + 1) * 512], scalar=FSC,
                            in1=GS[:, gs * 512:(gs + 1) * 512], op0=ALU.mult, op1=ALU.mult), ve_d)
                        note_read({bu}, ve_d)
                        gs_free[gslot] = [(ve_d, ve_d.n)]
                        h_ready[(hh, l5)] = ve_d.n

                if os.environ.get("KSUB", "full") == "gate":
                    raise _StopBuild()
                wait("tensor", g_wd[0], g_wd[1])
                for l5 in range(2):
                    _order = ([(s, dg, hh) for s in range(4) for dg in range(2)
                               for hh in range(NHH)] if not os.environ.get("KOLDDOWN") else
                              [(s, dg, hh) for hh in range(NHH) for s in range(4)
                               for dg in range(2)])
                    if os.environ.get("KOLDDOWN"):
                        wait_banks("tensor", set(range(8)))
                    for (s, dg, hh) in _order:
                        b = s * 2 + dg
                        if hh == 0 and not os.environ.get("KOLDDOWN"):
                            wait_banks("tensor", {b})
                        if True:
                            if True:
                                wait("tensor", ve_d, h_ready[(hh, l5)])
                                run("tensor", lambda E, hh=hh, s=s, dg=dg, b=b, l5=l5: E.matmul(
                                    PS[:, b * 512:(b + 1) * 512],
                                    AR[:, A_HB + hh * 1024 + l5 * 512 + s * 128:
                                       A_HB + hh * 1024 + l5 * 512 + (s + 1) * 128],
                                    AR[:, A_WD + hh * 1024 + dg * 512: A_WD + hh * 1024 + (dg + 1) * 512],
                                    start=(hh == 0), stop=(hh == NHH - 1)),
                                    t_mm if hh == NHH - 1 else None)
                    t_down = t_mm.n
                    if os.environ.get("KSUB", "full") == "downmm":
                        wait("vector", t_mm, t_down)
                        run("vector", lambda E: E.memset(ST[:, 0:1], 0.0), ve_d)
                        raise _StopBuild()
                    wait("vector", t_mm, t_down)
                    for s in range(4):
                        run("vector", lambda E, s=s: E.reduce_sum(
                            ST[:, S_SUM + s:S_SUM + s + 1], PS[:, s * 1024:(s + 1) * 1024], AX.X), ve_d)
                    if os.environ.get("KSUB", "full") == "red":
                        raise _StopBuild()
                    wait("scalar", t_mm, t_down)
                    wait("scalar", ve_d)
                    for s in range(4):
                        wait("scalar", sc_d, sq_last[0])
                        run("scalar", lambda E, s=s: E.activation(
                            out=SQ[:], in_=PS[:, s * 1024:(s + 1) * 1024], func=AF.Square,
                            scale=FSC, accum_out=ST[:, S_SSQ + s:S_SSQ + s + 1]), sc_d)
                        sq_last[0] = sc_d.n
                    if os.environ.get("KSUB", "full") == "sq":
                        raise _StopBuild()
                    wait("vector", sc_d)
                    ln_finalize(4, f_mean=FSC / 1024.0, f_sq=1.0 / 1024.0, rs_extra=FSC)
                    fin4_v = ve_d.n
                    if os.environ.get("KSUB", "full") == "ln4":
                        raise _StopBuild()
                    for s in range(4):
                        lt = lg * 8 + l5 * 4 + s
                        c = xload(lt, s % 2, src=x1_h)
                        for (sem, cnt) in trr_free[s % 2]:
                            wait("scalar", sem, cnt)
                        trr_free[s % 2] = []
                        wait("scalar", ve_d, fin4_v)
                        run("scalar", lambda E, s=s: E.activation(
                            out=TRR[:, (s % 2) * 1024:((s % 2) + 1) * 1024],
                            in_=PS[:, s * 1024:(s + 1) * 1024],
                            func=AF.Identity, bias=ST[:, S_NB + s:S_NB + s + 1],
                            scale=ST[:, S_RS + s:S_RS + s + 1]), sc_d)
                        note_read({s * 2, s * 2 + 1}, sc_d)
                        wait("vector", c[0], c[1])
                        wait("vector", sc_d)
                        for (sem, cnt) in or_free[s % 2]:
                            wait("vector", sem, cnt)
                        or_free[s % 2] = []
                        run("vector", lambda E, s=s: E.tensor_tensor(
                            out=ORR[:, (s % 2) * 1024:((s % 2) + 1) * 1024],
                            in0=XR[:, (s % 2) * 1024:((s % 2) + 1) * 1024],
                            in1=TRR[:, (s % 2) * 1024:((s % 2) + 1) * 1024], op=ALU.add), ve_d)
                        xr_free[s % 2] = [(ve_d, ve_d.n)]
                        trr_free[s % 2] = [(ve_d, ve_d.n)]
                        wait("sync", ve_d)
                        run("sync", lambda E, lt=lt, s=s: E.dma_start(
                            out=out_h[lt * 128:(lt + 1) * 128, :],
                            in_=ORR[:, (s % 2) * 1024:((s % 2) + 1) * 1024]), s_out, 16)
                        or_free[s % 2] = [(s_out, s_out.n)]
                    st_free[0] = sc_d.n

        except _StopBuild:
            pass
        wait("sync", s_dbg, s_dbg.n)
        wait("sync", s_out, s_out.n)
        wait("sync", s_x1, s_x1.n)
        wait("sync", s_sh, s_sh.n)
        wait("sync", sx[0], sx[0].n)
        wait("sync", sx[1], sx[1].n)
        wait("gpsimd", gsem[0], gsem[0].n)
        wait("gpsimd", gsem[1], gsem[1].n)

        block = ctx.enter_context(nc.Block())

        @block.tensor
        def _(E):
            for f in Qs["tensor"]:
                f(E)

        @block.scalar
        def _(E):
            for f in Qs["scalar"]:
                f(E)

        @block.vector
        def _(E):
            for f in Qs["vector"]:
                f(E)

        @block.gpsimd
        def _(E):
            for f in Qs["gpsimd"]:
                f(E)

        @block.sync
        def _(E):
            for f in Qs["sync"]:
                f(E)

    return nc


_NC_CACHE = None


def _get_nc():
    global _NC_CACHE
    if _NC_CACHE is None:
        _NC_CACHE = build_nc()
    return _NC_CACHE


def make_inputs(**inputs):
    x = np.asarray(inputs["L0_attn_x"], dtype=np.float32)
    wq = np.asarray(inputs["L0_attn_Wq"], dtype=np.float32)
    bq = np.asarray(inputs["L0_attn_bq"], dtype=np.float32)
    wk = np.asarray(inputs["L0_attn_Wk"], dtype=np.float32)
    bk = np.asarray(inputs["L0_attn_bk"], dtype=np.float32)
    wv = np.asarray(inputs["L0_attn_Wv"], dtype=np.float32)
    bv = np.asarray(inputs["L0_attn_bv"], dtype=np.float32)
    wo = np.asarray(inputs["L0_attn_Wo"], dtype=np.float32)
    bo = np.asarray(inputs["L0_attn_bo"], dtype=np.float32)
    wg = np.asarray(inputs["L0_ffn_w_gate"], dtype=np.int32)
    wu = np.asarray(inputs["L0_ffn_w_up"], dtype=np.int32)
    wd = np.asarray(inputs["L0_ffn_w_down"], dtype=np.int32)

    shared = {
        "wqT": np.ascontiguousarray(wq.T),
        "wkT": np.ascontiguousarray(wk.T),
        "wvT": np.ascontiguousarray(wv.T),
        "woT": np.ascontiguousarray(wo.T),
        "bqk": np.ascontiguousarray(
            np.concatenate([bq.reshape(8, 128).T, bk.reshape(8, 128).T], axis=1)),
        "bvr": np.ascontiguousarray(np.broadcast_to(bv[None, :], (128, D))),
        "bor": np.ascontiguousarray(np.broadcast_to(bo[None, :], (128, D))),
        "wgT": np.ascontiguousarray(wg.T),
        "wuT": np.ascontiguousarray(wu.T),
        "wdT": np.ascontiguousarray(wd.T),
        "tri": np.triu(np.ones((128, 128), dtype=np.float32)).astype(ml_dtypes.bfloat16),
        "idt": np.eye(128, dtype=ml_dtypes.bfloat16),
    }
    in_maps = []
    for b in range(8):
        m = dict(shared)
        m["x"] = np.ascontiguousarray(x[b])
        in_maps.append(m)
    return in_maps


def kernel(**inputs):
    in_maps = make_inputs(**inputs)
    nc = _get_nc()
    res = run_bass_kernel_spmd(nc, in_maps, core_ids=list(range(8)))
    out = np.stack([np.asarray(r["out"], dtype=np.float32) for r in res.results], axis=0)
    return out


# version 08 (final)
# speedup vs baseline: 1.3887x; 1.0324x over previous
import os
import numpy as np
import ml_dtypes
import concourse.bass as bass
import concourse.mybir as mybir
from concourse.bass_utils import run_bass_kernel_spmd
from concourse.ap import AP as APc
from contextlib import ExitStack

F32 = mybir.dt.float32
BF16 = mybir.dt.bfloat16
I32 = mybir.dt.int32
AF = mybir.ActivationFunctionType
ALU = mybir.AluOpType
AX = mybir.AxisListType

D = 1024
L = 2048
H = 16
DK = 64
HID = 3072
NE = 8
NL = 16
NG = 4
NKB = 4
NQC = 4
NP = 8
NHH = 24
NLG = 2
EPS = 1e-5
FSC = 1.0 / 127.0

AR_TOT = 65792
A_XNT = 0
A_Q = 16384
A_K = 32768
A_V = 49152
A_PB = 0
A_AS = 5120
A_OS = 9216
A_X2T = 0
A_HB = 8192
A_WD = 32768

QC_OFF = {0: 0, 1: 640, 2: 1024, 3: 512}
QC_W = {0: 512, 1: 384, 2: 256, 3: 128}

S_SUM, S_SSQ, S_MEAN, S_VAR, S_RS, S_NB = 0, 16, 32, 48, 64, 80


def build_nc(stages=None):
    if stages is None:
        stages = int(os.environ.get("KSTAGES", "4"))
    nc = bass.Bass("TRN2", target_bir_lowering=False, detect_race_conditions=False)
    x_h = nc.dram_tensor("x", [L, D], F32, kind="ExternalInput").ap()
    wqT_h = nc.dram_tensor("wqT", [D, D], F32, kind="ExternalInput").ap()
    wkT_h = nc.dram_tensor("wkT", [D, D], F32, kind="ExternalInput").ap()
    wvT_h = nc.dram_tensor("wvT", [D, D], F32, kind="ExternalInput").ap()
    woT_h = nc.dram_tensor("woT", [D, D], F32, kind="ExternalInput").ap()
    bqk_h = nc.dram_tensor("bqk", [128, 16], F32, kind="ExternalInput").ap()
    bvr_h = nc.dram_tensor("bvr", [128, D], F32, kind="ExternalInput").ap()
    bor_h = nc.dram_tensor("bor", [128, D], F32, kind="ExternalInput").ap()
    wgT_h = nc.dram_tensor("wgT", [D, HID], I32, kind="ExternalInput").ap()
    wuT_h = nc.dram_tensor("wuT", [D, HID], I32, kind="ExternalInput").ap()
    wdT_h = nc.dram_tensor("wdT", [HID, D], I32, kind="ExternalInput").ap()
    tri_h = nc.dram_tensor("tri", [128, 128], BF16, kind="ExternalInput").ap()
    idt_h = nc.dram_tensor("idt", [128, 128], BF16, kind="ExternalInput").ap()
    x1_kind = "ExternalOutput" if os.environ.get("KDBG") else "Internal"
    x1_h = nc.dram_tensor("x1s", [L, D], F32, kind=x1_kind).ap()
    dbg_h = None
    dbg2_h = None
    if os.environ.get("KDBG"):
        dbg_h = nc.dram_tensor("dbg", [128, 16384], BF16, kind="ExternalOutput").ap()
        dbg2_h = nc.dram_tensor("dbg2", [128, 96], F32, kind="ExternalOutput").ap()
        dbg3_h = nc.dram_tensor("dbg3", [128, 2048], F32, kind="ExternalOutput").ap()
    out_h = nc.dram_tensor("out", [L, D], F32, kind="ExternalOutput").ap()

    ctx = ExitStack()
    with ctx:
        def sb(name, shape, dt=F32):
            return ctx.enter_context(nc.sbuf_tensor(name, shape, dt))

        ST = sb("ST", [128, 96], F32)
        AR = sb("AR", [128, AR_TOT], BF16)
        WA = sb("WA", [128, 16384], BF16)
        XR = sb("XR", [128, 3072], F32)
        ORR = sb("ORR", [128, 2048], F32)
        XS = sb("XS", [128, 2048], BF16)
        TRR = sb("TRR", [128, 2048], BF16)
        GS = sb("GS", [128, 1024], BF16)
        ASTG = sb("ASTG", [64, 1024], BF16)
        RREP = sb("RREP", [64, 1024], BF16)
        BV = sb("BV", [128, D], BF16)
        BOr = sb("BOr", [128, D], BF16)
        BQK = sb("BQK", [128, 16], F32)
        TRI = sb("TRI", [128, 128], BF16)
        IDT = sb("IDT", [128, 128], BF16)
        ONES = sb("ONES", [1, 64], BF16)
        REC = sb("REC", [1, 1024], F32)
        RECB = sb("RECB", [1, 1024], BF16)
        SQ = sb("SQ", [128, 1024], BF16)
        PS = ctx.enter_context(nc.psum_tensor("PS", [128, 4096], F32))
        PSB = PS[:].bitcast(BF16)

        class Sem:
            def __init__(self, name):
                self.name = name
                self.h = ctx.enter_context(nc.semaphore(name))
                self.n = 0

        gsem = [Sem("g0"), Sem("g1"), Sem("g2"), Sem("g3")]
        gprev = [0, 0, 0, 0]
        gpar = [0]
        sx = [Sem("sx0"), Sem("sx1"), Sem("sx2")]
        s_sh = Sem("s_sh")
        s_x1 = Sem("s_x1")
        s_out = Sem("s_out")
        s_dbg = Sem("s_dbg")
        t_mm = Sem("t_mm")
        sc_d = Sem("sc_d")
        ve_d = Sem("ve_d")

        Qs = {k: [] for k in ("tensor", "scalar", "vector", "gpsimd", "sync")}
        waited = {k: {} for k in Qs}

        def emit(eng, fn):
            Qs[eng].append(fn)

        def wait(eng, sem, n=None):
            v = sem.n if n is None else n
            if v <= 0 or waited[eng].get(sem.name, 0) >= v:
                return
            waited[eng][sem.name] = v
            emit(eng, lambda E, h=sem.h, vv=v: E.wait_ge(h, vv))

        own = {"vector": ve_d, "scalar": sc_d}

        def run(eng, fn, inc=None, k=1, nf=False):
            o = own.get(eng)
            if o is not None:
                if not nf:
                    wait(eng, o, o.n)
                if inc is None:
                    inc, k = o, 1
            if inc is None:
                emit(eng, fn)
            else:
                emit(eng, lambda E, f=fn, h=inc.h, kk=k: f(E).then_inc(h, kk))
                inc.n += k

        ps_free = {b: [] for b in range(8)}

        def note_read(banks, sem):
            for b in banks:
                ps_free[b].append((sem, sem.n))

        def wait_banks(eng, banks):
            need = {}
            for b in banks:
                for (sem, cnt) in ps_free[b]:
                    need[sem] = max(need.get(sem, 0), cnt)
                ps_free[b] = []
            for sem, cnt in need.items():
                wait(eng, sem, cnt)

        def banks_of(c0, w):
            return set(range(c0 // 512, (c0 + w - 1) // 512 + 1))

        def heartbeat():
            if os.environ.get("KNOHB"):
                return
            wait("tensor", sc_d, sc_d.n)
            run("tensor", lambda E: E.matmul(PS[0:64, 2048:2112], ONES[:], ONES[:],
                                             start=True, stop=True), None)

        def ggroup(fns):
            s = gsem[gpar[0] % 4]
            wait("gpsimd", s, gprev[gpar[0] % 4])
            for f in fns:
                run("gpsimd", f, s, 16)
            gprev[gpar[0] % 4] = s.n
            gpar[0] += 1
            return (s, s.n)

        _skip = os.environ.get("KSKIP", "")
        g_const = ggroup([
            lambda E: E.dma_start(out=TRI[:], in_=tri_h),
            lambda E: E.dma_start(out=IDT[:], in_=idt_h),
            lambda E: E.dma_start(out=BQK[:], in_=bqk_h),
            lambda E: E.dma_start(out=BV[:], in_=bvr_h),
            lambda E: E.dma_start(out=BOr[:], in_=bor_h),
        ]) if "const" not in _skip else (gsem[0], 0)

        def load_w1024(src_h, slot):
            fns = []
            for half in range(2):
                def f(E, s=src_h, sl=slot, hf=half):
                    inp = s.rearrange("(dc p) e -> p dc e", p=128)[:, hf * 4:(hf + 1) * 4, :]
                    outp = WA[:, sl * 8192 + hf * 4096: sl * 8192 + (hf + 1) * 4096]
                    return E.dma_start(out=outp.rearrange("p (dc e) -> p dc e", e=1024), in_=inp)
                fns.append(f)
            return ggroup(fns)

        if "wqk" not in _skip:
            g_wq = load_w1024(wqT_h, 0)
            g_wk = load_w1024(wkT_h, 1)
        else:
            g_wq = g_wk = (gsem[0], 0)

        run("vector", lambda E: E.memset(ONES[:], 1.0), ve_d)

        def f_vones(E):
            v = AR[:, A_V:A_V + 16 * 1040].rearrange("p (t h c) -> p t h c", h=16, c=65)
            return E.memset(v[:, :, :, 64:65], 1.0)
        if "vones" not in _skip:
            run("vector", f_vones, ve_d)

        st_free = [0]
        sq_last = [0]

        xr_free = [[], [], []]

        def xload(tile, slot, src=x_h):
            for (sem, cnt) in xr_free[slot]:
                wait("sync", sem, cnt)
            xr_free[slot] = []
            run("sync", lambda E, t=tile, s=slot, sh=src: E.dma_start(
                out=XR[:, s * 1024:(s + 1) * 1024], in_=sh[t * 128:(t + 1) * 128, :]), sx[slot], 16)
            return (sx[slot], sx[slot].n)

        def ln_finalize(nt, f_mean=1.0 / 1024.0, f_sq=1.0 / 1024.0, rs_extra=None):
            _selfw = os.environ.get("KFINW")
            wait("vector", sc_d, st_free[0])
            run("vector", lambda E, n=nt, f=f_mean: E.tensor_scalar_mul(
                ST[:, S_MEAN:S_MEAN + n], ST[:, S_SUM:S_SUM + n], f), ve_d)
            if _selfw:
                wait("vector", ve_d)
            run("vector", lambda E, n=nt, f=f_sq: E.tensor_scalar(
                ST[:, S_VAR:S_VAR + n], ST[:, S_SSQ:S_SSQ + n], f, EPS, ALU.mult, ALU.add), ve_d)
            if _selfw:
                wait("vector", ve_d)
            run("vector", lambda E, n=nt: E.tensor_tensor(
                ST[:, S_RS:S_RS + n], ST[:, S_MEAN:S_MEAN + n], ST[:, S_MEAN:S_MEAN + n],
                ALU.mult), ve_d)
            if _selfw:
                wait("vector", ve_d)
            run("vector", lambda E, n=nt: E.tensor_tensor(
                ST[:, S_VAR:S_VAR + n], ST[:, S_VAR:S_VAR + n], ST[:, S_RS:S_RS + n],
                ALU.subtract), ve_d)
            if os.environ.get("KFIN") and nt == 16:
                wait("sync", ve_d, ve_d.n)
                run("sync", lambda E: E.dma_start(out=out_h[1408:1536, 0:96], in_=ST[:]), s_out, 16)
                wait("scalar", s_out, s_out.n)
            wait("scalar", ve_d)
            run("scalar", lambda E, n=nt: E.activation(
                out=ST[:, S_NB:S_NB + n], in_=ST[:, S_VAR:S_VAR + n], func=AF.Ln), sc_d)
            run("scalar", lambda E, n=nt: E.activation(
                out=ST[:, S_RS:S_RS + n], in_=ST[:, S_NB:S_NB + n], func=AF.Exp,
                scale=-0.5), sc_d)
            wait("vector", sc_d)
            run("vector", lambda E, n=nt: E.tensor_tensor(
                ST[:, S_NB:S_NB + n], ST[:, S_MEAN:S_MEAN + n], ST[:, S_RS:S_RS + n],
                ALU.mult), ve_d)
            run("vector", lambda E, n=nt: E.tensor_scalar_mul(
                ST[:, S_NB:S_NB + n], ST[:, S_NB:S_NB + n], -1.0), ve_d)
            if rs_extra is not None:
                run("vector", lambda E, n=nt, r=rs_extra: E.tensor_scalar_mul(
                    ST[:, S_RS:S_RS + n], ST[:, S_RS:S_RS + n], r), ve_d)

        class _StopBuild(Exception):
            pass
        try:
            for t in range(NL):
                c = xload(t, t % 3)
                if dbg2_h is not None and t == 1 and os.environ.get("KDBG2"):
                    wait("sync", sx[0], sx[0].n)
                    wait("sync", sx[1], sx[1].n)
                    run("sync", lambda E: E.dma_start(out=dbg3_h, in_=XR[:]), s_dbg, 16)
                if os.environ.get("KDBG2") and t >= 2:
                    wait("sync", s_dbg, 16)
                wait("vector", c[0], c[1])
                run("vector", lambda E, t=t, s=t % 3: E.reduce_sum(
                    ST[:, S_SUM + t:S_SUM + t + 1], XR[:, s * 1024:(s + 1) * 1024], AX.X), ve_d)
                wait("scalar", c[0], c[1])
                wait("scalar", sc_d, sq_last[0])
                run("scalar", lambda E, t=t, s=t % 3: E.activation(
                    out=SQ[:], in_=XR[:, s * 1024:(s + 1) * 1024], func=AF.Square,
                    accum_out=ST[:, S_SSQ + t:S_SSQ + t + 1]), sc_d)
                sq_last[0] = sc_d.n
                xr_free[t % 3] = [(ve_d, ve_d.n), (sc_d, sc_d.n)]
                heartbeat()

            wait("vector", sc_d)
            if "fin" not in _skip:
                ln_finalize(16)
            fin1_v = ve_d.n
            if os.environ.get("KSTD"):
                wait("sync", ve_d, ve_d.n)
                wait("sync", sc_d, sc_d.n)
                run("sync", lambda E: E.dma_start(out=out_h[0:128, 0:96], in_=ST[:]), s_out, 16)

            xs_free = [[], []]
            psb_free = [[], []]
            for t in range(NL if "p2" not in _skip else 0):
                c = xload(t, t % 3)
                slot = t % 2
                xslot = t % 3
                for (sem, cnt) in xs_free[slot]:
                    wait("scalar", sem, cnt)
                xs_free[slot] = []
                wait("scalar", c[0], c[1])
                wait("scalar", ve_d, fin1_v)
                run("scalar", lambda E, t=t, s=slot, xs=xslot: E.activation(
                    out=XS[:, s * 1024:(s + 1) * 1024], in_=XR[:, xs * 1024:(xs + 1) * 1024],
                    func=AF.Identity, bias=ST[:, S_NB + t:S_NB + t + 1],
                    scale=ST[:, S_RS + t:S_RS + t + 1]), sc_d)
                xr_free[xslot] = [(sc_d, sc_d.n)]
                wait("tensor", sc_d)
                for (sem, cnt) in psb_free[slot]:
                    wait("tensor", sem, cnt)
                psb_free[slot] = []
                for dc in range(NE):
                    run("tensor", lambda E, s=slot, dc=dc: E.transpose(
                        PSB[:, (6 + s) * 1024 + dc * 128: (6 + s) * 1024 + (dc + 1) * 128],
                        XS[:, s * 1024 + dc * 128: s * 1024 + (dc + 1) * 128], IDT[:]),
                        t_mm if dc == NE - 1 else None)
                xs_free[slot] = [(t_mm, t_mm.n)]
                if os.environ.get("KSCEVAC"):
                    wait("scalar", t_mm)

                    def fev(E, t=t, s=slot):
                        dst = APc(AR[:].tensor, t * 128, [[AR_TOT, 128], [2048, 8], [1, 128]])
                        src = PSB[:, (6 + s) * 1024:(7 + s) * 1024].rearrange("p (c l) -> p c l", l=128)
                        return E.activation(out=dst, in_=src, func=AF.Identity)
                    run("scalar", fev, sc_d)
                    psb_free[slot] = [(sc_d, sc_d.n)]
                    run("vector", lambda E: E.memset(ONES[0:1, 0:1], 1.0), ve_d)
                else:
                    wait("vector", t_mm)

                    def fev(E, t=t, s=slot):
                        dst = APc(AR[:].tensor, t * 128, [[AR_TOT, 128], [2048, 8], [1, 128]])
                        src = PSB[:, (6 + s) * 1024:(7 + s) * 1024].rearrange("p (c l) -> p c l", l=128)
                        return E.tensor_copy(dst, src)
                    run("vector", fev, ve_d)
                    psb_free[slot] = [(ve_d, ve_d.n)]
                    heartbeat()
            xnt_ready = ve_d.n
            st_free[0] = sc_d.n
            if dbg_h is not None:
                wait("sync", ve_d, xnt_ready)
                run("sync", lambda E: E.dma_start(out=dbg_h[:, 0:2048],
                                                  in_=AR[:, 0:2048]), s_dbg, 16)
            if stages <= 1:
                raise _StopBuild()

            def proj_arr1(slot, dst_base, bias_col, sem_w):
                for cc in range(NE):
                    for gq in range(NG):
                        b = (cc * NG + gq) % 4
                        wait("tensor", sem_w[0], sem_w[1])
                        wait("tensor", ve_d, xnt_ready)
                        wait_banks("tensor", {b})
                        for dc in range(NE):
                            run("tensor", lambda E, cc=cc, gq=gq, dc=dc, b=b, sl=slot: E.matmul(
                                PS[:, b * 512:(b + 1) * 512],
                                WA[:, sl * 8192 + dc * 1024 + cc * 128: sl * 8192 + dc * 1024 + (cc + 1) * 128],
                                AR[:, A_XNT + dc * 2048 + gq * 512: A_XNT + dc * 2048 + (gq + 1) * 512],
                                start=(dc == 0), stop=(dc == NE - 1)),
                                t_mm if dc == NE - 1 else None)
                        wait("vector", t_mm)
                        run("vector", lambda E, cc=cc, gq=gq, b=b, dst=dst_base, bc=bias_col:
                            E.tensor_scalar_add(
                                AR[:, dst + cc * 2048 + gq * 512: dst + cc * 2048 + (gq + 1) * 512],
                                PS[:, b * 512:(b + 1) * 512], BQK[:, bc + cc:bc + cc + 1]), ve_d,
                            nf=True)
                        note_read({b}, ve_d)

            if "proj" not in _skip:
                proj_arr1(0, A_Q, 0, g_wq)
            t_q_done = t_mm.n
            if "proj" not in _skip:
                proj_arr1(1, A_K, 8, g_wk)
            t_k_done = t_mm.n

            if "wvwo" not in _skip:
                wait("gpsimd", t_mm, t_q_done)
                g_wv = load_w1024(wvT_h, 0)
                wait("gpsimd", t_mm, t_k_done)
                g_wo = load_w1024(woT_h, 1)
            else:
                g_wv = g_wo = (gsem[0], 0)

            for t in range(NL if "proj" not in _skip else 0):
                for eg in range(2):
                    b = (t * 2 + eg) % 4
                    wait("tensor", g_wv[0], g_wv[1])
                    wait_banks("tensor", {b})
                    for dc in range(NE):
                        run("tensor", lambda E, t=t, eg=eg, dc=dc, b=b: E.matmul(
                            PS[:, b * 512:(b + 1) * 512],
                            AR[:, A_XNT + dc * 2048 + t * 128: A_XNT + dc * 2048 + (t + 1) * 128],
                            WA[:, dc * 1024 + eg * 512: dc * 1024 + (eg + 1) * 512],
                            start=(dc == 0), stop=(dc == NE - 1)),
                            t_mm if dc == NE - 1 else None)
                    wait("vector", t_mm)

                    def fv(E, t=t, eg=eg, b=b):
                        dst = APc(AR[:].tensor, A_V + t * 1040 + eg * 8 * 65,
                                  [[AR_TOT, 128], [65, 8], [1, 64]])
                        src = PS[:, b * 512:(b + 1) * 512].rearrange("p (h c) -> p h c", c=64)
                        b1 = APc(BV[:].tensor, eg * 512, [[D, 128], [64, 8], [1, 64]])
                        return E.scalar_tensor_tensor(out=dst, in0=src, scalar=1.0, in1=b1,
                                                      op0=ALU.mult, op1=ALU.add)
                    run("vector", fv, ve_d, nf=True)
                    note_read({b}, ve_d)
            v_all_ready = ve_d.n
            if os.environ.get("KQKV"):
                wait("gpsimd", ve_d, v_all_ready)
                run("gpsimd", lambda E: E.dma_start(out=out_h[0:128, :], in_=AR[:, A_Q:A_Q + 1024]), s_out, 16)
                run("gpsimd", lambda E: E.dma_start(out=out_h[128:256, :], in_=AR[:, A_K:A_K + 1024]), s_out, 16)
                run("gpsimd", lambda E: E.dma_start(out=out_h[256:384, :], in_=AR[:, A_V:A_V + 1024]), s_out, 16)
                run("gpsimd", lambda E: E.dma_start(out=out_h[384:512, :], in_=AR[:, A_XNT:A_XNT + 1024]), s_out, 16)
                wait("gpsimd", s_out, s_out.n)
                run("gpsimd", lambda E: E.dma_start(out=out_h[512:640, :], in_=AR[:, A_Q:A_Q + 1024]), s_out, 16)
                run("gpsimd", lambda E: E.dma_start(out=out_h[640:768, :], in_=AR[:, A_K:A_K + 1024]), s_out, 16)
                run("gpsimd", lambda E: E.dma_start(out=out_h[768:896, :], in_=AR[:, A_V:A_V + 1024]), s_out, 16)
                run("gpsimd", lambda E: E.dma_start(out=out_h[896:1024, :], in_=AR[:, A_XNT:A_XNT + 1024]), s_out, 16)
                run("gpsimd", lambda E: E.dma_start(out=out_h[1024:1152, :], in_=XS[:, 0:1024]), s_out, 16)
                run("gpsimd", lambda E: E.dma_start(out=out_h[1152:1280, :], in_=XS[:, 1024:2048]), s_out, 16)
                wait("sync", ve_d, ve_d.n)
                wait("sync", sc_d, sc_d.n)
                run("sync", lambda E: E.dma_start(out=out_h[1280:1408, 0:96], in_=ST[:]), s_out, 16)
                wait("gpsimd", s_out, s_out.n - 16)
                wait("sync", s_out, s_out.n)
                raise _StopBuild()
            if dbg_h is not None:
                wait("sync", ve_d, v_all_ready)
                run("sync", lambda E: E.dma_start(out=dbg_h[:, 2048:4096],
                                                  in_=AR[:, A_Q:A_Q + 2048]), s_dbg, 16)
                run("sync", lambda E: E.dma_start(out=dbg_h[:, 4096:6144],
                                                  in_=AR[:, A_K:A_K + 2048]), s_dbg, 16)
                run("sync", lambda E: E.dma_start(out=dbg_h[:, 6144:7184],
                                                  in_=AR[:, A_V:A_V + 1040]), s_dbg, 16)
                wait("scalar", s_dbg, s_dbg.n)
            if stages <= 2:
                raise _StopBuild()

            REG = [0, 1536]
            ATA, ATB = 3072, 3584
            reg_guard = {0: [], 1: []}
            at_guard = []
            pb_guard = {0: [], 1: [], 2: [], 3: []}
            astg_free = [[], []]
            trr_free = [[], []]
            or_free = [[], []]
            shift_cnt = 0
            x1_store_cnt = {}
            first_scores = True

            for g in range(NG):
                for p in range(NP):
                    for kb in range(NKB):
                        regA = kb % 2
                        regB = 1 - regA
                        for rg in (regA, regB):
                            for (sem, cnt) in reg_guard[rg]:
                                wait("tensor", sem, cnt)
                            reg_guard[rg] = []
                        if first_scores:
                            wait("tensor", ve_d, v_all_ready)
                            first_scores = False
                        for qc in range(NQC):
                            off = QC_OFF[qc]
                            wdt = QC_W[qc]
                            for (rg, lo, hi, tp) in ((regA, 0, 64, (0, 0)), (regB, 64, 128, (64, 0))):
                                run("tensor", lambda E, p=p, g=g, kb=kb, qc=qc, rg=rg, lo=lo, hi=hi,
                                    tp=tp, off=off, wdt=wdt: E.matmul(
                                        PS[:, REG[rg] + off: REG[rg] + off + wdt],
                                        AR[lo:hi, A_K + p * 2048 + kb * 512 + qc * 128:
                                           A_K + p * 2048 + kb * 512 + (qc + 1) * 128],
                                        AR[lo:hi, A_Q + p * 2048 + g * 512 + qc * 128:
                                           A_Q + p * 2048 + g * 512 + 512],
                                        start=True, stop=True, tile_position=tp),
                                    t_mm if (qc == NQC - 1 and rg == regB) else None)
                        t_scores = t_mm.n
                        ve_mask_h = {}
                        for (rg, buf) in ((regA, kb % 2), (regB, 2 + kb % 2)):
                            for (sem, cnt) in pb_guard[buf]:
                                wait("scalar", sem, cnt)
                            pb_guard[buf] = []
                            wait("scalar", t_mm, t_scores)
                            run("scalar", lambda E, rg=rg, buf=buf: E.activation(
                                out=AR[:, A_PB + buf * 1280: A_PB + (buf + 1) * 1280],
                                in_=PS[:, REG[rg]: REG[rg] + 1280], func=AF.Exp, scale=0.125), sc_d,
                                nf=True)
                            note_read(banks_of(REG[rg], 1280), sc_d)
                            reg_guard[rg].append((sc_d, sc_d.n))
                            sc_exp_h = sc_d.n
                            wait("vector", sc_d, sc_exp_h)
                            for (mb, stride) in ((0, 640), (512, 512)):
                                def fm(E, buf=buf, mb=mb, stride=stride):
                                    vv = APc(AR[:].tensor, A_PB + buf * 1280 + mb,
                                             [[AR_TOT, 128], [stride, 2], [1, 128]])
                                    tb = TRI[:].unsqueeze(1).broadcast_to([128, 2, 128])
                                    return E.tensor_tensor(out=vv, in0=vv, in1=tb, op=ALU.mult)
                                run("vector", fm, ve_d, nf=True)
                            ve_mask_h[buf] = ve_d.n
                        if kb == 0:
                            for (sem, cnt) in at_guard:
                                wait("tensor", sem, cnt)
                            at_guard = []
                        for (hb, buf, atc) in ((0, kb % 2, ATA), (1, 2 + kb % 2, ATB)):
                            wait("tensor", ve_d, ve_mask_h[buf])
                            hh = 2 * p + hb
                            for qc in range(NQC):
                                jt = kb * NQC + qc
                                run("tensor", lambda E, hh=hh, jt=jt, qc=qc, buf=buf, atc=atc,
                                    kb=kb: E.matmul(
                                        PS[0:65, atc + qc * 128: atc + 512],
                                        AR[:, A_V + jt * 1040 + hh * 65: A_V + jt * 1040 + hh * 65 + 65],
                                        AR[:, A_PB + buf * 1280 + QC_OFF[qc]:
                                           A_PB + buf * 1280 + QC_OFF[qc] + QC_W[qc]],
                                        start=(kb == 0 and qc == 0),
                                        stop=(kb == NKB - 1 and qc == NQC - 1)),
                                    t_mm if (kb == NKB - 1 and qc == NQC - 1) else None)
                            pb_guard[buf] = [(t_mm, t_mm.n)]
                    t_pv = t_mm.n
                    wait("scalar", t_mm, t_pv)
                    run("scalar", lambda E: E.activation(
                        out=REC[0:1, 0:1024], in_=PS[64:65, ATA:ATA + 1024],
                        func=AF.Ln), sc_d)
                    run("scalar", lambda E: E.activation(
                        out=RECB[0:1, 0:1024], in_=REC[0:1, 0:1024],
                        func=AF.Exp, scale=-1.0), sc_d)
                    sc_rec = sc_d.n
                    wait("tensor", sc_d, sc_rec)
                    for (sem, cnt) in reg_guard[0]:
                        wait("tensor", sem, cnt)
                    reg_guard[0] = []
                    run("tensor", lambda E: E.matmul(PS[0:64, 0:512], ONES[:], RECB[0:1, 0:512],
                        start=True, stop=True), None)
                    run("tensor", lambda E: E.matmul(PS[0:64, 512:1024], ONES[:], RECB[0:1, 512:1024],
                        start=True, stop=True), t_mm)
                    t_r1 = t_mm.n
                    wait("vector", t_mm, t_r1)
                    run("vector", lambda E: E.tensor_copy(RREP[0:64, 0:512], PS[0:64, 0:512]), ve_d)
                    run("vector", lambda E: E.tensor_copy(RREP[0:64, 512:1024], PS[0:64, 512:1024]), ve_d)
                    reg_guard[0].append((ve_d, ve_d.n))
                    run("vector", lambda E, p=p: E.tensor_tensor(
                        out=AR[0:64, A_AS + p * 512: A_AS + (p + 1) * 512],
                        in0=PS[0:64, ATA:ATA + 512], in1=RREP[0:64, 0:512], op=ALU.mult), ve_d)
                    for (sem, cnt) in astg_free[p % 2]:
                        wait("vector", sem, cnt)
                    astg_free[p % 2] = []
                    run("vector", lambda E, p=p: E.tensor_tensor(
                        out=ASTG[0:64, (p % 2) * 512:(p % 2 + 1) * 512],
                        in0=PS[0:64, ATB:ATB + 512], in1=RREP[0:64, 512:1024], op=ALU.mult), ve_d)
                    at_guard = [(ve_d, ve_d.n)]
                    ve_norm = ve_d.n
                    wait("sync", ve_d, ve_norm)
                    run("sync", lambda E, p=p: E.dma_start(
                        out=AR[64:128, A_AS + p * 512: A_AS + (p + 1) * 512],
                        in_=ASTG[0:64, (p % 2) * 512:(p % 2 + 1) * 512]), s_sh, 16)
                    shift_cnt = s_sh.n
                    astg_free[p % 2] = [(s_sh, s_sh.n)]

                wait("tensor", s_sh, shift_cnt)
                wait("tensor", g_wo[0], g_wo[1])
                for rg in (0, 1):
                    for (sem, cnt) in reg_guard[rg]:
                        wait("tensor", sem, cnt)
                    reg_guard[rg] = []
                for (sem, cnt) in at_guard:
                    wait("tensor", sem, cnt)
                at_guard = []
                wait_banks("tensor", set(range(8)))
                for s in range(4):
                    for eg in range(2):
                        b = s * 2 + eg
                        for cc in range(NP):
                            run("tensor", lambda E, s=s, eg=eg, cc=cc, b=b: E.matmul(
                                PS[:, b * 512:(b + 1) * 512],
                                AR[:, A_AS + cc * 512 + s * 128: A_AS + cc * 512 + (s + 1) * 128],
                                WA[:, 8192 + cc * 1024 + eg * 512: 8192 + cc * 1024 + (eg + 1) * 512],
                                start=(cc == 0), stop=(cc == NP - 1)),
                                t_mm if cc == NP - 1 else None)
                        wait("vector", t_mm)

                        def fo(E, s=s, eg=eg, b=b):
                            return E.scalar_tensor_tensor(
                                out=AR[:, A_OS + s * 1024 + eg * 512: A_OS + s * 1024 + (eg + 1) * 512],
                                in0=PS[:, b * 512:(b + 1) * 512], scalar=1.0,
                                in1=BOr[:, eg * 512:(eg + 1) * 512], op0=ALU.mult, op1=ALU.add)
                        run("vector", fo, ve_d, nf=True)
                        note_read({b}, ve_d)
                ve_oevac = ve_d.n
                reg_guard[0].append((ve_d, ve_oevac))
                reg_guard[1].append((ve_d, ve_oevac))
                at_guard.append((ve_d, ve_oevac))

                wait("vector", ve_d, ve_oevac)
                for s in range(4):
                    run("vector", lambda E, s=s: E.reduce_sum(
                        ST[:, S_SUM + s:S_SUM + s + 1],
                        AR[:, A_OS + s * 1024: A_OS + (s + 1) * 1024], AX.X), ve_d)
                wait("scalar", ve_d, ve_oevac)
                for s in range(4):
                    wait("scalar", sc_d, sq_last[0])
                    run("scalar", lambda E, s=s: E.activation(
                        out=SQ[:], in_=AR[:, A_OS + s * 1024: A_OS + (s + 1) * 1024],
                        func=AF.Square, accum_out=ST[:, S_SSQ + s:S_SSQ + s + 1]), sc_d)
                    sq_last[0] = sc_d.n
                wait("vector", sc_d)
                ln_finalize(4)
                fin2_v = ve_d.n
                for s in range(4):
                    t_ = 4 * g + s
                    c = xload(t_, s % 2)
                    for (sem, cnt) in trr_free[s % 2]:
                        wait("scalar", sem, cnt)
                    trr_free[s % 2] = []
                    wait("scalar", ve_d, fin2_v)
                    run("scalar", lambda E, s=s: E.activation(
                        out=TRR[:, (s % 2) * 1024:((s % 2) + 1) * 1024],
                        in_=AR[:, A_OS + s * 1024: A_OS + (s + 1) * 1024],
                        func=AF.Identity, bias=ST[:, S_NB + s:S_NB + s + 1],
                        scale=ST[:, S_RS + s:S_RS + s + 1]), sc_d)
                    wait("vector", c[0], c[1])
                    wait("vector", sc_d)
                    for (sem, cnt) in or_free[s % 2]:
                        wait("vector", sem, cnt)
                    or_free[s % 2] = []
                    run("vector", lambda E, s=s: E.tensor_tensor(
                        out=ORR[:, (s % 2) * 1024:((s % 2) + 1) * 1024],
                        in0=XR[:, (s % 2) * 1024:((s % 2) + 1) * 1024],
                        in1=TRR[:, (s % 2) * 1024:((s % 2) + 1) * 1024], op=ALU.add), ve_d)
                    xr_free[s % 2] = [(ve_d, ve_d.n)]
                    trr_free[s % 2] = [(ve_d, ve_d.n)]
                    wait("sync", ve_d)
                    run("sync", lambda E, t_=t_, s=s: E.dma_start(
                        out=x1_h[t_ * 128:(t_ + 1) * 128, :],
                        in_=ORR[:, (s % 2) * 1024:((s % 2) + 1) * 1024]), s_x1, 16)
                    x1_store_cnt[t_] = s_x1.n
                    or_free[s % 2] = [(s_x1, s_x1.n)]
                st_free[0] = sc_d.n

            t_attn_done = t_mm.n
            attn_ve_done = ve_d.n
            attn_sc_done = sc_d.n
            if stages <= 3:
                wait("sync", s_x1, s_x1.n)
                run("sync", lambda E: E.dma_start(out=out_h, in_=x1_h), s_out, 16)
                raise _StopBuild()

            wait("gpsimd", t_mm, t_attn_done)
            _fwd = []
            for i6 in range(6):
                def fwd(E, i6=i6):
                    inp = wdT_h.rearrange("(hh p) d -> p hh d", p=128)[:, i6 * 4:(i6 + 1) * 4, :]
                    outp = AR[:, A_WD + i6 * 4096: A_WD + (i6 + 1) * 4096]
                    return E.dma_start(out=outp.rearrange("p (hh d) -> p hh d", d=1024), in_=inp)
                _fwd.append(fwd)
            g_wd = ggroup(_fwd)

            wait("tensor", ve_d, attn_ve_done)
            wait("tensor", sc_d, attn_sc_done)

            wa_guard = {r: [] for r in range(16)}
            wa_init = [(t_mm, t_attn_done)]
            gs_free = [[], []]

            for lg in range(NLG):
                for t in range(8):
                    lt = lg * 8 + t
                    wait("sync", s_x1, x1_store_cnt[lt])
                    c = xload(lt, t % 3, src=x1_h)
                    wait("vector", c[0], c[1])
                    run("vector", lambda E, t=t, s=t % 3: E.reduce_sum(
                        ST[:, S_SUM + t:S_SUM + t + 1], XR[:, s * 1024:(s + 1) * 1024], AX.X), ve_d)
                    wait("scalar", c[0], c[1])
                    wait("scalar", sc_d, sq_last[0])
                    run("scalar", lambda E, t=t, s=t % 3: E.activation(
                        out=SQ[:], in_=XR[:, s * 1024:(s + 1) * 1024], func=AF.Square,
                        accum_out=ST[:, S_SSQ + t:S_SSQ + t + 1]), sc_d)
                    sq_last[0] = sc_d.n
                    xr_free[t % 3] = [(ve_d, ve_d.n), (sc_d, sc_d.n)]
                    heartbeat()
                wait("vector", sc_d)
                ln_finalize(8)
                fin3_v = ve_d.n
                for t in range(8):
                    lt = lg * 8 + t
                    c = xload(lt, t % 3, src=x1_h)
                    slot = t % 2
                    xslot = t % 3
                    for (sem, cnt) in xs_free[slot]:
                        wait("scalar", sem, cnt)
                    xs_free[slot] = []
                    wait("scalar", c[0], c[1])
                    wait("scalar", ve_d, fin3_v)
                    run("scalar", lambda E, t=t, s=slot, xs=xslot: E.activation(
                        out=XS[:, s * 1024:(s + 1) * 1024], in_=XR[:, xs * 1024:(xs + 1) * 1024],
                        func=AF.Identity, bias=ST[:, S_NB + t:S_NB + t + 1],
                        scale=ST[:, S_RS + t:S_RS + t + 1]), sc_d)
                    xr_free[xslot] = [(sc_d, sc_d.n)]
                    wait("tensor", sc_d)
                    for (sem, cnt) in psb_free[slot]:
                        wait("tensor", sem, cnt)
                    psb_free[slot] = []
                    wait_banks("tensor", {slot})
                    for dc in range(NE):
                        run("tensor", lambda E, s=slot, dc=dc: E.transpose(
                            PSB[:, s * 1024 + dc * 128: s * 1024 + (dc + 1) * 128],
                            XS[:, s * 1024 + dc * 128: s * 1024 + (dc + 1) * 128], IDT[:]),
                            t_mm if dc == NE - 1 else None)
                    xs_free[slot] = [(t_mm, t_mm.n)]
                    wait("vector", t_mm)

                    def fev2(E, t=t, s=slot):
                        dst = APc(AR[:].tensor, A_X2T + t * 128, [[AR_TOT, 128], [1024, 8], [1, 128]])
                        src = PSB[:, s * 1024:(s + 1) * 1024].rearrange("p (c l) -> p c l", l=128)
                        return E.tensor_copy(dst, src)
                    run("vector", fev2, ve_d)
                    psb_free[slot] = [(ve_d, ve_d.n)]
                    heartbeat()
                    note_read({slot}, ve_d)
                x2t_ready = ve_d.n
                st_free[0] = sc_d.n
                if os.environ.get("KSUB", "full") == "ln3":
                    raise _StopBuild()

                h_ready = {}
                for hh in range(NHH):
                    rg_ = hh % 8
                    ru_ = 8 + hh % 8
                    for r_ in (rg_, ru_):
                        for (sem, cnt) in (wa_guard[r_] or wa_init):
                            wait("gpsimd", sem, cnt)
                        wa_guard[r_] = []
                    _fsl = []
                    for (src_h, r_) in ((wgT_h, rg_), (wuT_h, ru_)):
                        for half in range(2):
                            def fsl(E, sh=src_h, r=r_, hf=half, hh=hh):
                                inp = sh.rearrange("(dc p) h -> p dc h", p=128)[
                                    :, hf * 4:(hf + 1) * 4, hh * 128:(hh + 1) * 128]
                                outp = WA[:, r * 1024 + hf * 512: r * 1024 + (hf + 1) * 512]
                                return E.dma_start(
                                    out=outp.rearrange("p (dc e) -> p dc e", e=128), in_=inp)
                            _fsl.append(fsl)
                    g_sl = ggroup(_fsl)
                    for l5 in range(2):
                        bg = (hh * 2 + l5) % 2
                        bu = 2 + bg
                        wait("tensor", g_sl[0], g_sl[1])
                        wait("tensor", ve_d, x2t_ready)
                        wait_banks("tensor", {bg, bu})
                        for dc in range(NE):
                            run("tensor", lambda E, dc=dc, bg=bg, r=rg_, l5=l5: E.matmul(
                                PS[:, bg * 512:(bg + 1) * 512],
                                WA[:, r * 1024 + dc * 128: r * 1024 + (dc + 1) * 128],
                                AR[:, A_X2T + dc * 1024 + l5 * 512: A_X2T + dc * 1024 + (l5 + 1) * 512],
                                start=(dc == 0), stop=(dc == NE - 1)),
                                t_mm if dc == NE - 1 else None)
                        t_g = t_mm.n
                        for dc in range(NE):
                            run("tensor", lambda E, dc=dc, bu=bu, r=ru_, l5=l5: E.matmul(
                                PS[:, bu * 512:(bu + 1) * 512],
                                WA[:, r * 1024 + dc * 128: r * 1024 + (dc + 1) * 128],
                                AR[:, A_X2T + dc * 1024 + l5 * 512: A_X2T + dc * 1024 + (l5 + 1) * 512],
                                start=(dc == 0), stop=(dc == NE - 1)),
                                t_mm if dc == NE - 1 else None)
                        t_u = t_mm.n
                        wa_guard[rg_] = [(t_mm, t_g)]
                        wa_guard[ru_] = [(t_mm, t_u)]
                        gslot = (hh * 2 + l5) % 2
                        for (sem, cnt) in gs_free[gslot]:
                            wait("scalar", sem, cnt)
                        gs_free[gslot] = []
                        wait("scalar", t_mm, t_g)
                        run("scalar", lambda E, bg=bg, gs=gslot: E.activation(
                            out=GS[:, gs * 512:(gs + 1) * 512], in_=PS[:, bg * 512:(bg + 1) * 512],
                            func=AF.Silu, scale=FSC), sc_d, nf=True)
                        note_read({bg}, sc_d)
                        wait("vector", sc_d)
                        wait("vector", t_mm, t_u)
                        run("vector", lambda E, hh=hh, l5=l5, bu=bu, gs=gslot: E.scalar_tensor_tensor(
                            out=AR[:, A_HB + hh * 1024 + l5 * 512: A_HB + hh * 1024 + (l5 + 1) * 512],
                            in0=PS[:, bu * 512:(bu + 1) * 512], scalar=FSC,
                            in1=GS[:, gs * 512:(gs + 1) * 512], op0=ALU.mult, op1=ALU.mult), ve_d,
                        nf=True)
                        note_read({bu}, ve_d)
                        gs_free[gslot] = [(ve_d, ve_d.n)]
                        h_ready[(hh, l5)] = ve_d.n

                if os.environ.get("KSUB", "full") == "gate":
                    raise _StopBuild()
                wait("tensor", g_wd[0], g_wd[1])
                for l5 in range(2):
                    _order = ([(s, dg, hh) for s in range(4) for dg in range(2)
                               for hh in range(NHH)] if not os.environ.get("KOLDDOWN") else
                              [(s, dg, hh) for hh in range(NHH) for s in range(4)
                               for dg in range(2)])
                    if os.environ.get("KOLDDOWN"):
                        wait_banks("tensor", set(range(8)))
                    for (s, dg, hh) in _order:
                        b = s * 2 + dg
                        if hh == 0 and not os.environ.get("KOLDDOWN"):
                            wait_banks("tensor", {b})
                        if True:
                            if True:
                                wait("tensor", ve_d, h_ready[(hh, l5)])
                                run("tensor", lambda E, hh=hh, s=s, dg=dg, b=b, l5=l5: E.matmul(
                                    PS[:, b * 512:(b + 1) * 512],
                                    AR[:, A_HB + hh * 1024 + l5 * 512 + s * 128:
                                       A_HB + hh * 1024 + l5 * 512 + (s + 1) * 128],
                                    AR[:, A_WD + hh * 1024 + dg * 512: A_WD + hh * 1024 + (dg + 1) * 512],
                                    start=(hh == 0), stop=(hh == NHH - 1)),
                                    t_mm if hh == NHH - 1 else None)
                    t_down = t_mm.n
                    if os.environ.get("KSUB", "full") == "downmm":
                        wait("vector", t_mm, t_down)
                        run("vector", lambda E: E.memset(ST[:, 0:1], 0.0), ve_d)
                        raise _StopBuild()
                    wait("vector", t_mm, t_down)
                    for s in range(4):
                        run("vector", lambda E, s=s: E.reduce_sum(
                            ST[:, S_SUM + s:S_SUM + s + 1], PS[:, s * 1024:(s + 1) * 1024], AX.X), ve_d)
                    if os.environ.get("KSUB", "full") == "red":
                        raise _StopBuild()
                    wait("scalar", t_mm, t_down)
                    wait("scalar", ve_d)
                    for s in range(4):
                        wait("scalar", sc_d, sq_last[0])
                        run("scalar", lambda E, s=s: E.activation(
                            out=SQ[:], in_=PS[:, s * 1024:(s + 1) * 1024], func=AF.Square,
                            scale=FSC, accum_out=ST[:, S_SSQ + s:S_SSQ + s + 1]), sc_d)
                        sq_last[0] = sc_d.n
                    if os.environ.get("KSUB", "full") == "sq":
                        raise _StopBuild()
                    wait("vector", sc_d)
                    ln_finalize(4, f_mean=FSC / 1024.0, f_sq=1.0 / 1024.0, rs_extra=FSC)
                    fin4_v = ve_d.n
                    if os.environ.get("KSUB", "full") == "ln4":
                        raise _StopBuild()
                    for s in range(4):
                        lt = lg * 8 + l5 * 4 + s
                        c = xload(lt, s % 2, src=x1_h)
                        for (sem, cnt) in trr_free[s % 2]:
                            wait("scalar", sem, cnt)
                        trr_free[s % 2] = []
                        wait("scalar", ve_d, fin4_v)
                        run("scalar", lambda E, s=s: E.activation(
                            out=TRR[:, (s % 2) * 1024:((s % 2) + 1) * 1024],
                            in_=PS[:, s * 1024:(s + 1) * 1024],
                            func=AF.Identity, bias=ST[:, S_NB + s:S_NB + s + 1],
                            scale=ST[:, S_RS + s:S_RS + s + 1]), sc_d)
                        note_read({s * 2, s * 2 + 1}, sc_d)
                        wait("vector", c[0], c[1])
                        wait("vector", sc_d)
                        for (sem, cnt) in or_free[s % 2]:
                            wait("vector", sem, cnt)
                        or_free[s % 2] = []
                        run("vector", lambda E, s=s: E.tensor_tensor(
                            out=ORR[:, (s % 2) * 1024:((s % 2) + 1) * 1024],
                            in0=XR[:, (s % 2) * 1024:((s % 2) + 1) * 1024],
                            in1=TRR[:, (s % 2) * 1024:((s % 2) + 1) * 1024], op=ALU.add), ve_d)
                        xr_free[s % 2] = [(ve_d, ve_d.n)]
                        trr_free[s % 2] = [(ve_d, ve_d.n)]
                        wait("sync", ve_d)
                        run("sync", lambda E, lt=lt, s=s: E.dma_start(
                            out=out_h[lt * 128:(lt + 1) * 128, :],
                            in_=ORR[:, (s % 2) * 1024:((s % 2) + 1) * 1024]), s_out, 16)
                        or_free[s % 2] = [(s_out, s_out.n)]
                    st_free[0] = sc_d.n

        except _StopBuild:
            pass
        wait("sync", s_dbg, s_dbg.n)
        wait("sync", s_out, s_out.n)
        wait("sync", s_x1, s_x1.n)
        wait("sync", s_sh, s_sh.n)
        wait("sync", sx[0], sx[0].n)
        wait("sync", sx[1], sx[1].n)
        wait("gpsimd", gsem[0], gsem[0].n)
        wait("gpsimd", gsem[1], gsem[1].n)

        block = ctx.enter_context(nc.Block())

        @block.tensor
        def _(E):
            for f in Qs["tensor"]:
                f(E)

        @block.scalar
        def _(E):
            for f in Qs["scalar"]:
                f(E)

        @block.vector
        def _(E):
            for f in Qs["vector"]:
                f(E)

        @block.gpsimd
        def _(E):
            for f in Qs["gpsimd"]:
                f(E)

        @block.sync
        def _(E):
            for f in Qs["sync"]:
                f(E)

    return nc


_NC_CACHE = None


def _get_nc():
    global _NC_CACHE
    if _NC_CACHE is None:
        _NC_CACHE = build_nc()
    return _NC_CACHE


def make_inputs(**inputs):
    x = np.asarray(inputs["L0_attn_x"], dtype=np.float32)
    wq = np.asarray(inputs["L0_attn_Wq"], dtype=np.float32)
    bq = np.asarray(inputs["L0_attn_bq"], dtype=np.float32)
    wk = np.asarray(inputs["L0_attn_Wk"], dtype=np.float32)
    bk = np.asarray(inputs["L0_attn_bk"], dtype=np.float32)
    wv = np.asarray(inputs["L0_attn_Wv"], dtype=np.float32)
    bv = np.asarray(inputs["L0_attn_bv"], dtype=np.float32)
    wo = np.asarray(inputs["L0_attn_Wo"], dtype=np.float32)
    bo = np.asarray(inputs["L0_attn_bo"], dtype=np.float32)
    wg = np.asarray(inputs["L0_ffn_w_gate"], dtype=np.int32)
    wu = np.asarray(inputs["L0_ffn_w_up"], dtype=np.int32)
    wd = np.asarray(inputs["L0_ffn_w_down"], dtype=np.int32)

    shared = {
        "wqT": np.ascontiguousarray(wq.T),
        "wkT": np.ascontiguousarray(wk.T),
        "wvT": np.ascontiguousarray(wv.T),
        "woT": np.ascontiguousarray(wo.T),
        "bqk": np.ascontiguousarray(
            np.concatenate([bq.reshape(8, 128).T, bk.reshape(8, 128).T], axis=1)),
        "bvr": np.ascontiguousarray(np.broadcast_to(bv[None, :], (128, D))),
        "bor": np.ascontiguousarray(np.broadcast_to(bo[None, :], (128, D))),
        "wgT": np.ascontiguousarray(wg.T),
        "wuT": np.ascontiguousarray(wu.T),
        "wdT": np.ascontiguousarray(wd.T),
        "tri": np.triu(np.ones((128, 128), dtype=np.float32)).astype(ml_dtypes.bfloat16),
        "idt": np.eye(128, dtype=ml_dtypes.bfloat16),
    }
    in_maps = []
    for b in range(8):
        m = dict(shared)
        m["x"] = np.ascontiguousarray(x[b])
        in_maps.append(m)
    return in_maps


def kernel(**inputs):
    in_maps = make_inputs(**inputs)
    nc = _get_nc()
    res = run_bass_kernel_spmd(nc, in_maps, core_ids=list(range(8)))
    out = np.stack([np.asarray(r["out"], dtype=np.float32) for r in res.results], axis=0)
    return out
